# version 01
# baseline (speedup 1.0000x reference)
import os
import numpy as np
import ml_dtypes
import concourse.bass as bass
import concourse.mybir as mybir
from concourse.bacc import Bacc
from concourse.bass_utils import run_bass_kernel_spmd

F32 = mybir.dt.float32
BF16 = mybir.dt.bfloat16
I32 = mybir.dt.int32
AF = mybir.ActivationFunctionType
ALU = mybir.AluOpType
AX = mybir.AxisListType.X
NB = np.dtype(ml_dtypes.bfloat16)

B_, T_, C_ = 2, 8192, 1024
H, DH, CS = 16, 64, 128
SEG, SEGP = 2048, 2049
NCH, NJ = 16, 8
H, TOPK = 16, 8
NEXP = 16
EPS = 1e-5
ROUTE_SCALE = 2.826
STAT_TILES = [(0, 512), (512, 512), (1024, 512), (1536, 512), (2048, 1)]
TOK_TILES = [(0, 512), (512, 512), (1024, 512), (1536, 512)]


class Ctr:
    def __init__(self, sem):
        self.sem = sem
        self.v = 0

    def inc(self, instr, n=1):
        instr.then_inc(self.sem, n)
        self.v += n
        return self.v


def host_consts():
    tri_l = np.tril(np.ones((128, 128), np.float32), -1)
    tri_u = np.triu(np.ones((128, 128), np.float32), 1)
    c = {}
    c["stril4"] = np.tile(tri_l, (1, 4)).astype(NB)
    c["striu4"] = np.tile(tri_u, (1, 4)).astype(NB)
    c["striuS4"] = np.tile(tri_u, (1, 4)).astype(NB)
    c["identb"] = np.eye(128, dtype=np.float32).astype(NB)
    c["id64b"] = np.tile(np.eye(64, dtype=np.float32), (2, 1)).astype(NB)
    c["id64_in"] = np.eye(64, dtype=np.float32)
    c["ones_in"] = np.ones((128, 128), np.float32)
    e16 = np.zeros((64, 16 * 16), np.float32)
    for h in range(16):
        e16[:, 16 * h + h] = 1.0
    c["e16"] = np.tile(e16, (2, 1)).astype(NB)
    e2 = np.zeros((16, 8 * 128), np.float32)
    for j in range(8):
        e2[2 * j, 128 * j:128 * j + 64] = 1.0
        e2[2 * j + 1, 128 * j + 64:128 * j + 128] = 1.0
    c["e2sel"] = e2.astype(NB)
    c["tri01"] = np.tril(np.ones((128, 128), np.float32)).T.astype(NB)
    c["ob128"] = np.ones((128, 128), np.float32).astype(NB)
    e3 = np.zeros((16, 16 * 128), np.float32)
    for e in range(16):
        e3[e, 128 * e:128 * e + 128] = 1.0
    c["e16sel"] = e3.astype(NB)
    c["id128f"] = np.eye(128, dtype=np.float32)
    return c


def build_launch_A(debug=(), phases=9):
    nc = bass.Bass("TRN2", target_bir_lowering=False)

    xs = nc.dram_tensor("xs", [C_, SEGP], F32, kind="ExternalInput").ap()
    w_in = {}
    for n in ["wq", "wk", "wv", "woT", "wrT", "wkkT", "wvvT"]:
        w_in[n] = nc.dram_tensor(n, [C_, C_], F32, kind="ExternalInput").ap()
    ab_raw = nc.dram_tensor("ab_raw", [1, 32], F32, kind="ExternalInput").ap()
    mu_in = nc.dram_tensor("mu_in", [128, 16], F32, kind="ExternalInput").ap()
    stril4 = nc.dram_tensor("stril4", [128, 512], BF16, kind="ExternalInput").ap()
    striu4 = nc.dram_tensor("striu4", [128, 512], BF16, kind="ExternalInput").ap()
    striuS4 = nc.dram_tensor("striuS4", [128, 512], BF16, kind="ExternalInput").ap()
    identb_in = nc.dram_tensor("identb", [128, 128], BF16, kind="ExternalInput").ap()
    id64b_in = nc.dram_tensor("id64b", [128, 64], BF16, kind="ExternalInput").ap()
    id64_in = nc.dram_tensor("id64_in", [64, 64], F32, kind="ExternalInput").ap()
    ones_in = nc.dram_tensor("ones_in", [128, 128], F32, kind="ExternalInput").ap()
    e16_in = nc.dram_tensor("e16", [128, 256], BF16, kind="ExternalInput").ap()
    e2sel_in = nc.dram_tensor("e2sel", [16, 1024], BF16, kind="ExternalInput").ap()

    x1p = nc.dram_tensor("x1p", [C_, SEG], F32, kind="ExternalOutput").ap()
    zbuf = nc.dram_tensor("zbuf", [C_, SEG], BF16, kind="ExternalOutput").ap()
    abuf = nc.dram_tensor("abuf", [C_, NCH * 64], F32, kind="ExternalOutput").ap()
    cbuf = nc.dram_tensor("cbuf", [C_, 128], F32, kind="ExternalOutput").ap()
    dbg_out = {}
    for name in debug:
        dbg_out[name] = nc.dram_tensor("dbg_" + name, [C_, SEG], F32,
                                       kind="ExternalOutput").ap()

    from contextlib import ExitStack
    with ExitStack() as _stk:
        E = _stk.enter_context
        xn = E(nc.sbuf_tensor("xn", [128, NJ * SEGP], BF16))
        q_fm = E(nc.sbuf_tensor("q_fm", [128, NJ * SEG], BF16))
        k_fm = E(nc.sbuf_tensor("k_fm", [128, NJ * SEG], BF16))
        v_fm = E(nc.sbuf_tensor("v_fm", [128, NJ * SEG], BF16))
        wt = E(nc.sbuf_tensor("wt", [128, NJ * 1024], BF16))
        xtmp = E(nc.sbuf_tensor("xtmp", [128, SEGP], F32))
        xld = E(nc.sbuf_tensor("xld", [128, 1024], F32))
        scr2 = E(nc.sbuf_tensor("scr2", [128, 1024], F32))
        rows = E(nc.sbuf_tensor("rows", [1, 2560], F32))
        sq16 = E(nc.sbuf_tensor("sq16", [128, 512], BF16))
        scrb = E(nc.sbuf_tensor("scrb", [128, 512], BF16))
        consts = E(nc.sbuf_tensor("consts", [128, 2048], BF16))
        ones1f = E(nc.sbuf_tensor("ones1f", [1, 128], F32))
        id64 = E(nc.sbuf_tensor("id64", [64, 64], F32))
        id64b = E(nc.sbuf_tensor("id64b_sb", [128, 64], BF16))
        e16 = E(nc.sbuf_tensor("e16_sb", [128, 256], BF16))
        e2sel = E(nc.sbuf_tensor("e2sel_sb", [16, 1024], BF16))
        rinvb = E(nc.sbuf_tensor("rinvb", [16, SEG], BF16))
        ab_sb = E(nc.sbuf_tensor("ab_sb", [1, 32], F32))
        abrow = E(nc.sbuf_tensor("abrow", [1, 32], F32))
        narow = E(nc.sbuf_tensor("narow", [1, 16], F32))
        mu_sb = E(nc.sbuf_tensor("mu_sb", [128, 16], F32))
        muc_sb = E(nc.sbuf_tensor("muc_sb", [128, 16], F32))
        bcol = E(nc.sbuf_tensor("bcol", [128, 8], F32))
        nahead = E(nc.sbuf_tensor("nahead", [128, 16], F32))
        norms = E(nc.sbuf_tensor("norms", [16, SEG], F32))
        epsc = E(nc.sbuf_tensor("epsc", [1, 1], F32))
        kvtm = E(nc.sbuf_tensor("kvtm", [128, 512], BF16))
        Xsb = E(nc.sbuf_tensor("Xsb", [128, 512], BF16))
        Mas = E(nc.sbuf_tensor("Mas", [128, 512], BF16))
        Mtas = E(nc.sbuf_tensor("Mtas", [128, 512], BF16))
        Mbs = E(nc.sbuf_tensor("Mbs", [128, 512], BF16))
        Mtbs = E(nc.sbuf_tensor("Mtbs", [128, 512], BF16))
        Hts = E(nc.sbuf_tensor("Hts", [128, 512], BF16))
        Zs = E(nc.sbuf_tensor("Zs", [64, 512], BF16))
        AhTs = E(nc.sbuf_tensor("AhTs", [64, 256], F32))
        Ntst = E(nc.sbuf_tensor("Ntst", [64, 64], F32))
        Atst = E(nc.sbuf_tensor("Atst", [64, 64], F32))
        Ntb = E(nc.sbuf_tensor("Ntb", [64, 64], BF16))
        qcp = E(nc.sbuf_tensor("qcp", [64, 512], BF16))
        pA = E(nc.psum_tensor("pA", [128, 512], F32))
        pB = E(nc.psum_tensor("pB", [128, 512], F32))
        pC = E(nc.psum_tensor("pC", [128, 512], F32))
        pD = E(nc.psum_tensor("pD", [128, 512], F32))
        pE = E(nc.psum_tensor("pE", [128, 512], F32))
        pT = E(nc.psum_tensor("pT", [128, 1024], BF16))
        pR = E(nc.psum_tensor("pR", [128, 512], F32))
        pS = E(nc.psum_tensor("pS", [64, 512], F32))
        s_g = E(nc.semaphore("s_g"))
        s_w = E(nc.semaphore("s_w"))
        s_mm = E(nc.semaphore("s_mm"))
        s_dv = E(nc.semaphore("s_dv"))
        s_ac = E(nc.semaphore("s_ac"))
        s_st = E(nc.semaphore("s_st"))
        s_wt = E(nc.semaphore("s_wt"))
        block = E(nc.Block())
        G = Ctr(s_g)
        W = Ctr(s_w)
        MM = Ctr(s_mm)
        DV = Ctr(s_dv)
        AC = Ctr(s_ac)
        ST = Ctr(s_st)
        WT = Ctr(s_wt)

        PE = block.tensor
        DVE = block.vector
        ACT = block.scalar
        GP = block.gpsimd
        SY = block.sync

        def xnf(j):
            return xn[:, j * SEGP:(j + 1) * SEGP]

        def qf(j):
            return q_fm[:, j * SEG:(j + 1) * SEG]

        def kf(j):
            return k_fm[:, j * SEG:(j + 1) * SEG]

        def vf(j):
            return v_fm[:, j * SEG:(j + 1) * SEG]

        def hslice(buf_f, h):
            r0 = 64 * (h % 2)
            return buf_f(h // 2)[r0:r0 + 64, :]

        m_stril = consts[:, 0:512]
        m_striu = consts[:, 512:1024]
        m_striuS = consts[:, 1024:1536]
        m_ident = consts[:, 1536:1664]
        m_ones1 = consts[:, 1664:1665]

        def wload(name):
            return w_in[name].rearrange("(kc p) o -> p kc o", p=128)

        def _(g):
            G.inc(g.dma_start(out=m_stril, in_=stril4), 16)
            G.inc(g.dma_start(out=m_striu, in_=striu4), 16)
            G.inc(g.dma_start(out=m_striuS, in_=striuS4), 16)
            G.inc(g.dma_start(out=m_ident, in_=identb_in), 16)
            G.inc(g.dma_start(out=id64[:, :], in_=id64_in), 16)
            G.inc(g.dma_start(out=id64b[:, :], in_=id64b_in), 16)
            G.inc(g.dma_start(out=e16[:, :], in_=e16_in), 16)
            G.inc(g.dma_start(out=e2sel[:, :], in_=e2sel_in), 16)
            G.inc(g.dma_start(out=ab_sb[:, :], in_=ab_raw), 16)
            G.inc(g.dma_start(out=mu_sb[:, 0:16], in_=mu_in), 16)
            G.inc(g.dma_start(out=ones1f[:, :], in_=ones_in[0:1, :]), 16)
            G.inc(g.dma_start(out=m_ones1, in_=ones_in[:, 0:1], allow_slow_non_contiguous=True), 16)
            WT.inc(g.dma_start(out=wt.ap().rearrange("p (kc o) -> p kc o", kc=NJ), in_=wload("wq")), 16)
        GP(_)
        LOADED_G = G.v

        for j in range(NJ):
            def _(s, j=j):
                if j > 0:
                    s.wait_ge(s_dv, DV.v)
                W.inc(s.dma_start(out=xtmp[:, :],
                                  in_=xs[128 * j:128 * j + 128, :]), 16)
            SY(_)

            def _(v, j=j):
                v.wait_ge(s_w, W.v)
                DV.inc(v.tensor_copy(xnf(j), xtmp[:, :]))
            DVE(_)
        XCOPIES_DV = DV.v

        def _(sc):
            sc.wait_ge(s_g, LOADED_G)
            AC.inc(sc.activation(abrow[:, :], ab_sb[:, :], AF.Sigmoid))
        ACT(_)

        def _(v):
            v.wait_ge(s_ac, AC.v)
            DV.inc(v.tensor_scalar_mul(abrow[:, :], abrow[:, :], 0.5))
            v.wait_ge(s_dv, DV.v)
            DV.inc(v.tensor_scalar_mul(narow[:, :], abrow[0:1, 0:16], -1.0))
            DV.inc(v.tensor_scalar(muc_sb[:, :], mu_sb[:, 0:16], -1.0, 1.0,
                                   ALU.mult, ALU.add))
            DV.inc(v.memset(epsc[:, :], EPS))
        DVE(_)

        def _(t):
            t.wait_ge(s_dv, DV.v)
            MM.inc(t.matmul(pR[:, 0:16], ones1f[:, :], narow[:, :],
                            start=True, stop=True))
            MM.inc(t.matmul(pR[:, 16:24], ones1f[:, :],
                            bass.AP(abrow, 16, [[32, 1], [2, 8]]),
                            start=True, stop=True))
            MM.inc(t.matmul(pR[:, 24:32], ones1f[:, :],
                            bass.AP(abrow, 17, [[32, 1], [2, 8]]),
                            start=True, stop=True))
        PE(_)

        def _(v):
            v.wait_ge(s_mm, MM.v)
            DV.inc(v.tensor_copy(nahead[:, :], pR[:, 0:16]))
            DV.inc(v.tensor_copy(bcol[0:64, :], pR[0:64, 16:24]))
            DV.inc(v.tensor_copy(bcol[64:128, :], pR[64:128, 24:32]))
        DVE(_)
        PREP_DV = DV.v

        r_sum = rows[:, 0:512]
        r_ssq = rows[:, 512:1024]
        r_negm = rows[:, 1024:1536]
        r_rstd = rows[:, 1536:2048]
        r_tmp = rows[:, 2048:2560]

        for ti, (c0, cn) in enumerate(STAT_TILES):
            def _(t, ti=ti, c0=c0, cn=cn):
                t.wait_ge(s_dv, max(XCOPIES_DV, PREP_DV))
                t.wait_ge(s_dv, DV.v)
                for j in range(NJ):
                    MM.inc(t.matmul(pA[0:1, 0:cn], m_ones1,
                                    xnf(j)[:, c0:c0 + cn],
                                    start=(j == 0), stop=(j == NJ - 1)))
            PE(_)
            SUMX_MM = MM.v

            for j in range(NJ):
                def _(sc, j=j, c0=c0, cn=cn):
                    sc.wait_ge(s_mm, MM.v)
                    AC.inc(sc.activation(sq16[:, 0:cn], xnf(j)[:, c0:c0 + cn],
                                         AF.Square))
                ACT(_)

                def _(t, j=j, cn=cn):
                    t.wait_ge(s_ac, AC.v)
                    MM.inc(t.matmul(pB[0:1, 0:cn], m_ones1, sq16[:, 0:cn],
                                    start=(j == 0), stop=(j == NJ - 1)))
                PE(_)

            def _(v, c0=c0, cn=cn):
                v.wait_ge(s_mm, MM.v)
                DV.inc(v.tensor_copy(r_sum[:, 0:cn], pA[0:1, 0:cn]))
                DV.inc(v.tensor_copy(r_ssq[:, 0:cn], pB[0:1, 0:cn]))
                v.wait_ge(s_dv, DV.v)
                DV.inc(v.tensor_scalar_mul(r_negm[:, 0:cn], r_sum[:, 0:cn],
                                           -1.0 / C_))
                v.wait_ge(s_dv, DV.v)
                DV.inc(v.tensor_mul(r_tmp[:, 0:cn], r_negm[:, 0:cn],
                                    r_negm[:, 0:cn]))
                DV.inc(v.tensor_scalar(r_rstd[:, 0:cn], r_ssq[:, 0:cn],
                                       1.0 / C_, None, ALU.mult, ALU.bypass))
                v.wait_ge(s_dv, DV.v)
                DV.inc(v.tensor_sub(r_rstd[:, 0:cn], r_rstd[:, 0:cn],
                                    r_tmp[:, 0:cn]))
            DVE(_)

            def _(sc, cn=cn):
                sc.wait_ge(s_dv, DV.v)
                AC.inc(sc.activation(r_tmp[:, 0:cn], r_rstd[:, 0:cn], AF.Sqrt,
                                     bias=epsc[:, :]))
            ACT(_)

            def _(v, cn=cn):
                v.wait_ge(s_ac, AC.v)
                DV.inc(v.reciprocal(r_rstd[:, 0:cn], r_tmp[:, 0:cn]))
            DVE(_)

            def _(t, cn=cn):
                t.wait_ge(s_dv, DV.v)
                MM.inc(t.matmul(pA[:, 0:cn], ones1f[:, :], r_negm[:, 0:cn],
                                start=True, stop=True))
                MM.inc(t.matmul(pB[:, 0:cn], ones1f[:, :], r_rstd[:, 0:cn],
                                start=True, stop=True))
            PE(_)

            def _(v, c0=c0, cn=cn):
                v.wait_ge(s_mm, MM.v)
                for j in range(NJ):
                    DV.inc(v.tensor_add(xnf(j)[:, c0:c0 + cn],
                                        xnf(j)[:, c0:c0 + cn], pA[:, 0:cn]))
                v.wait_ge(s_dv, DV.v)
                for j in range(NJ):
                    DV.inc(v.tensor_mul(xnf(j)[:, c0:c0 + cn],
                                        xnf(j)[:, c0:c0 + cn], pB[:, 0:cn]))
            DVE(_)
        LN_DONE = DV.v

        if phases < 3:
            def _(s):
                s.wait_ge(s_dv, DV.v)
                W.inc(s.dma_start(out=x1p[0:128, 0:512], in_=xld[:, 0:512]), 16)
                s.wait_ge(s_w, W.v)
                s.wait_ge(s_g, G.v)
            SY(_)
            return nc
        def emit_proj(evac, extra_waits=()):
            pp = [pA, pB]
            idx = [0]
            marks = []
            entry_dv = DV.v
            entry_ac = AC.v
            for oc in range(NJ):
                for (c0, cn) in TOK_TILES:
                    p = pp[idx[0] % 2]

                    def _(t, oc=oc, c0=c0, cn=cn, p=p, i=idx[0]):
                        if i < 2:
                            for sem, val in extra_waits:
                                t.wait_ge(sem, val)
                            t.wait_ge(s_dv, entry_dv)
                            t.wait_ge(s_ac, entry_ac)
                        else:
                            t.wait_ge(s_dv, marks[i - 2])
                        for kc in range(NJ):
                            MM.inc(t.matmul(
                                p[:, 0:cn],
                                wt[:, kc * 1024 + oc * 128:kc * 1024 + oc * 128 + 128],
                                xnf(kc)[:, 1 + c0:1 + c0 + cn],
                                start=(kc == 0), stop=(kc == NJ - 1)))
                    PE(_)

                    def _(v, oc=oc, c0=c0, cn=cn, p=p):
                        v.wait_ge(s_mm, MM.v)
                        evac(v, oc, c0, cn, p)
                        marks.append(DV.v)
                    DVE(_)
                    idx[0] += 1

        def ev_plain(dst_f):
            def f(v, oc, c0, cn, p):
                DV.inc(v.tensor_copy(dst_f(oc)[:, c0:c0 + cn], p[:, 0:cn]))
            return f

        def ev_bscale(dst_f):
            def f(v, oc, c0, cn, p):
                DV.inc(v.tensor_scalar_mul(dst_f(oc)[:, c0:c0 + cn], p[:, 0:cn],
                                           bcol[:, oc:oc + 1]))
            return f

        emit_proj(ev_plain(qf), [(s_dv, LN_DONE), (s_wt, 16), (s_g, LOADED_G)])
        if phases == 31:
            def _(s):
                s.wait_ge(s_dv, DV.v)
                s.wait_ge(s_wt, WT.v)
                s.wait_ge(s_g, G.v)
                s.wait_ge(s_w, W.v)
            SY(_)
            return nc

        def wswap(name):
            def _(g, name=name):
                g.wait_ge(s_mm, MM.v)
                WT.inc(g.dma_start(out=wt.ap().rearrange("p (kc o) -> p kc o", kc=NJ), in_=wload(name)), 16)
            GP(_)
            return WT.v

        WK_RDY = wswap("wk")
        emit_proj(ev_plain(kf), [(s_wt, WK_RDY)])
        if phases == 32:
            def _(s):
                s.wait_ge(s_dv, DV.v)
                s.wait_ge(s_wt, WT.v)
                s.wait_ge(s_g, G.v)
                s.wait_ge(s_w, W.v)
            SY(_)
            return nc
        WV_RDY = wswap("wv")
        emit_proj((ev_plain if phases == 33 else ev_bscale)(vf), [(s_wt, WV_RDY)])
        V_DONE = DV.v
        if phases in (33, 34):
            def _(s):
                s.wait_ge(s_dv, DV.v)
                s.wait_ge(s_wt, WT.v)
                s.wait_ge(s_g, G.v)
                s.wait_ge(s_w, W.v)
            SY(_)
            return nc
        WO_RDY = wswap("woT")
        if phases == 35:
            def _(s):
                s.wait_ge(s_dv, DV.v)
                s.wait_ge(s_wt, WT.v)
                s.wait_ge(s_g, G.v)
                s.wait_ge(s_w, W.v)
            SY(_)
            return nc

        for tt, (c0, cn) in enumerate(TOK_TILES):
            for j in range(NJ):
                def _(sc, j=j, c0=c0, cn=cn, tt=tt):
                    if j == 0 and tt == 0:
                        sc.wait_ge(s_dv, V_DONE)
                    sc.wait_ge(s_mm, MM.v)
                    AC.inc(sc.activation(sq16[:, 0:cn], kf(j)[:, c0:c0 + cn],
                                         AF.Square))
                ACT(_)

                def _(t, j=j, cn=cn):
                    t.wait_ge(s_ac, AC.v)
                    if j == 0:
                        t.wait_ge(s_dv, DV.v)
                    MM.inc(t.matmul(pS[0:16, 0:cn],
                                    e16[0:64, 16 * (2 * j):16 * (2 * j) + 16],
                                    sq16[0:64, 0:cn],
                                    start=(j == 0), stop=(j == NJ - 1)))
                    MM.inc(t.matmul(pR[32:48, 0:cn],
                                    e16[64:128, 16 * (2 * j + 1):16 * (2 * j + 1) + 16],
                                    sq16[64:128, 0:cn],
                                    start=(j == 0), stop=(j == NJ - 1)))
                PE(_)

            def _(v, c0=c0, cn=cn):
                v.wait_ge(s_mm, MM.v)
                DV.inc(v.tensor_copy(norms[:, c0:c0 + cn], pS[0:16, 0:cn]))
                v.wait_ge(s_dv, DV.v)
                DV.inc(v.tensor_add(norms[:, c0:c0 + cn], norms[:, c0:c0 + cn],
                                    pR[32:48, 0:cn]))
            DVE(_)

        if phases == 40:
            def _(s):
                s.wait_ge(s_dv, DV.v)
                s.wait_ge(s_wt, WT.v)
                s.wait_ge(s_g, G.v)
                s.wait_ge(s_w, W.v)
            SY(_)
            return nc

        def _(sc):
            sc.wait_ge(s_dv, DV.v)
            AC.inc(sc.activation(norms[:, :], norms[:, :], AF.Sqrt))
        ACT(_)

        def _(v):
            v.wait_ge(s_ac, AC.v)
            DV.inc(v.tensor_scalar_max(norms[:, :], norms[:, :], 1e-12))
            v.wait_ge(s_dv, DV.v)
            DV.inc(v.reciprocal(norms[:, :], norms[:, :]))
            v.wait_ge(s_dv, DV.v)
            DV.inc(v.tensor_copy(rinvb[:, :], norms[:, :]))
        DVE(_)
        RINV_DV = DV.v

        if phases == 41:
            def _(s):
                s.wait_ge(s_dv, DV.v)
                s.wait_ge(s_wt, WT.v)
                s.wait_ge(s_g, G.v)
                s.wait_ge(s_w, W.v)
            SY(_)
            return nc

        for j in range(NJ):
            for tt, (c0, cn) in enumerate(TOK_TILES):
                def _(t, j=j, c0=c0, cn=cn):
                    t.wait_ge(s_dv, DV.v)
                    MM.inc(t.matmul(pR[:, 0:cn], e2sel[:, 128 * j:128 * j + 128],
                                    rinvb[:, c0:c0 + cn],
                                    start=True, stop=True))
                PE(_)

                def _(v, j=j, c0=c0, cn=cn):
                    v.wait_ge(s_mm, MM.v)
                    DV.inc(v.tensor_mul(kf(j)[:, c0:c0 + cn], kf(j)[:, c0:c0 + cn],
                                        pR[:, 0:cn]))
                DVE(_)
        KHAT_DV = DV.v

        def dump_dbg(name, buf, pitch, off):
            if name not in dbg_out:
                return

            def _(g, name=name, buf=buf, pitch=pitch, off=off):
                g.wait_ge(s_dv, DV.v)
                g.wait_ge(s_ac, AC.v)
                G.inc(g.dma_start(
                    out=dbg_out[name].rearrange("(j p) t -> p j t", p=128),
                    in_=bass.AP(buf, off, [[NJ * pitch, 128],
                                                  [pitch, NJ], [1, SEG]])), 16)
            GP(_)

        dump_dbg("xn", xn, SEGP, 1)
        dump_dbg("q", q_fm, SEG, 0)
        dump_dbg("k", k_fm, SEG, 0)

        if phases < 5:
            def _(s):
                s.wait_ge(s_dv, DV.v)
                s.wait_ge(s_ac, AC.v)
                s.wait_ge(s_w, W.v)
                s.wait_ge(s_g, G.v)
                s.wait_ge(s_wt, WT.v)
            SY(_)
            return nc

        def wave(h, w):
            qh = hslice(qf, h)
            kh = hslice(kf, h)
            vh = hslice(vf, h)
            nah = nahead[:, h:h + 1]
            nah64 = nahead[0:64, h:h + 1]

            def _(t, h=h, w=w):
                t.wait_ge(s_dv, DV.v)
                t.wait_ge(s_ac, AC.v)
                r0 = 64 * (h % 2)
                idb = id64b[r0:r0 + 64, :]
                for c in range(4):
                    cg = 4 * w + c
                    MM.inc(t.transpose(pT[:, c * 128:c * 128 + 64],
                                       vh[:, cg * 128:cg * 128 + 128], idb))
                    MM.inc(t.transpose(pT[:, c * 128 + 64:c * 128 + 128],
                                       kh[:, cg * 128:cg * 128 + 128], idb))
            PE(_)
            TP_MM = MM.v

            def _(v):
                v.wait_ge(s_mm, TP_MM)
                DV.inc(v.tensor_copy(kvtm[:, :], pT[:, 0:512]))
            DVE(_)

            def _(sc):
                sc.wait_ge(s_mm, TP_MM)
                AC.inc(sc.activation(Xsb[:, :], pT[:, 0:512], AF.Identity))
            ACT(_)
            KV_DV, KV_AC = DV.v, AC.v

            def _(t, h=h, w=w):
                t.wait_ge(s_dv, KV_DV)
                t.wait_ge(s_ac, KV_AC)
                for c in range(4):
                    cg = 4 * w + c
                    ks = kh[:, cg * 128:cg * 128 + 128]
                    MM.inc(t.matmul(pA[:, c * 128:c * 128 + 128], ks, ks,
                                    start=True, stop=True))
                for c in range(4):
                    cg = 4 * w + c
                    MM.inc(t.matmul(pB[:, c * 128:c * 128 + 128],
                                    kh[:, cg * 128:cg * 128 + 128],
                                    qh[:, cg * 128:cg * 128 + 128],
                                    start=True, stop=True))
            PE(_)
            G_MM = MM.v

            def _(v, h=h):
                v.wait_ge(s_mm, G_MM)
                DV.inc(v.scalar_tensor_tensor(Mas[:, :], pA[:, :], nah,
                                              m_stril, ALU.mult, ALU.mult))
                DV.inc(v.scalar_tensor_tensor(Mtas[:, :], pA[:, :], nah,
                                              m_striu, ALU.mult, ALU.mult))
                DV.inc(v.tensor_mul(Hts[:, :], pB[:, :], m_striuS))
            DVE(_)

            cur = (Mas, Mtas)
            nxt = (Mbs, Mtbs)
            last_dv = DV.v
            last_ac = AC.v
            for li, p in enumerate([1, 2, 4, 8, 16, 32, 64]):
                def _(t, cur=cur, p=p, last_dv=last_dv, last_ac=last_ac):
                    t.wait_ge(s_dv, last_dv)
                    t.wait_ge(s_ac, last_ac)
                    curM, curMt = cur
                    for c in range(4):
                        MM.inc(t.matmul(pA[:, c * 128:c * 128 + 128],
                                        curMt[:, c * 128:c * 128 + 128],
                                        Xsb[:, c * 128:c * 128 + 128],
                                        start=True, stop=True))
                    if p < 32:
                        for c in range(4):
                            MM.inc(t.matmul(pB[:, c * 128:c * 128 + 128],
                                            curMt[:, c * 128:c * 128 + 128],
                                            curM[:, c * 128:c * 128 + 128],
                                            start=True, stop=True))
                        for c in range(4):
                            MM.inc(t.matmul(pC[:, c * 128:c * 128 + 128],
                                            curM[:, c * 128:c * 128 + 128],
                                            curMt[:, c * 128:c * 128 + 128],
                                            start=True, stop=True))
                    elif p == 32:
                        for c in range(4):
                            MM.inc(t.matmul(pB[:, c * 128:c * 128 + 128],
                                            curM[:, c * 128:c * 128 + 128],
                                            curMt[:, c * 128:c * 128 + 128],
                                            start=True, stop=True))
                PE(_)
                LVL_MM = MM.v

                def _(v, LVL_MM=LVL_MM):
                    v.wait_ge(s_mm, LVL_MM)
                    DV.inc(v.tensor_add(Xsb[:, :], pA[:, :], Xsb[:, :]))
                DVE(_)

                if p < 32:
                    def _(sc, nxt=nxt, LVL_MM=LVL_MM):
                        sc.wait_ge(s_mm, LVL_MM)
                        AC.inc(sc.activation(nxt[0][:, :], pB[:, :], AF.Identity))
                        AC.inc(sc.activation(nxt[1][:, :], pC[:, :], AF.Identity))
                    ACT(_)
                elif p == 32:
                    def _(sc, nxt=nxt, LVL_MM=LVL_MM):
                        sc.wait_ge(s_mm, LVL_MM)
                        AC.inc(sc.activation(nxt[1][:, :], pB[:, :], AF.Identity))
                    ACT(_)
                last_dv, last_ac = DV.v, AC.v
                cur, nxt = nxt, cur

            def _(t, last_dv=last_dv, last_ac=last_ac):
                t.wait_ge(s_dv, last_dv)
                t.wait_ge(s_ac, last_ac)
                for c in range(4):
                    xk = Xsb[:, c * 128 + 64:c * 128 + 128]
                    MM.inc(t.matmul(pD[0:64, c * 128:c * 128 + 128], xk,
                                    Hts[:, c * 128:c * 128 + 128],
                                    start=True, stop=True))
                    MM.inc(t.matmul(pE[0:64, c * 64:c * 64 + 64], xk,
                                    kvtm[:, c * 128 + 64:c * 128 + 128],
                                    start=True, stop=True))
            PE(_)
            ZH_MM = MM.v

            def _(v, h=h, w=w, ZH_MM=ZH_MM):
                v.wait_ge(s_mm, ZH_MM)
                v.wait_ge(s_g, G.v)
                qsl = hslice(qf, h)[:, (4 * w) * 128:(4 * w) * 128 + 512]
                if h % 2 == 1:
                    DV.inc(v.tensor_copy(qcp[:, :], qsl))
                    v.wait_ge(s_dv, DV.v)
                    qsl = qcp[:, :]
                DV.inc(v.scalar_tensor_tensor(
                    Zs[:, :], pD[0:64, :], nah64, qsl,
                    ALU.mult, ALU.add))
                DV.inc(v.tensor_scalar_mul(AhTs[:, :], pE[0:64, 0:256], nah64))
            DVE(_)
            ZA_DV = DV.v

            def _(g, h=h, w=w, ZA_DV=ZA_DV):
                g.wait_ge(s_dv, ZA_DV)
                G.inc(g.dma_start(out=zbuf[64 * h:64 * h + 64,
                                           512 * w:512 * w + 512],
                                  in_=Zs[:, :]), 16)
                G.inc(g.dma_start(out=abuf[64 * h:64 * h + 64,
                                           256 * w:256 * w + 256],
                                  in_=AhTs[:, :]), 16)
            GP(_)

            for c in range(4):
                cg = 4 * w + c

                def _(v):
                    v.wait_ge(s_dv, DV.v)
                    v.wait_ge(s_mm, MM.v)
                    DV.inc(v.tensor_copy(Ntb[:, :], Ntst[:, :]))
                DVE(_)
                NTB_DV = DV.v

                def _(t, c=c, NTB_DV=NTB_DV):
                    t.wait_ge(s_dv, NTB_DV)
                    kt = kvtm[:, c * 128 + 64:c * 128 + 128]
                    xv = Xsb[:, c * 128:c * 128 + 64]
                    ah = AhTs[:, c * 64:c * 64 + 64]
                    MM.inc(t.matmul(pS[:, 0:64], kt, xv, start=True, stop=False))
                    MM.inc(t.matmul(pS[:, 0:64], ah, Ntst[:, :],
                                    start=False, stop=False))
                    MM.inc(t.matmul(pS[:, 0:64], id64[:, :], Ntst[:, :],
                                    start=False, stop=True))
                    MM.inc(t.matmul(pS[:, 64:128], ah, Atst[:, :],
                                    start=True, stop=False))
                    MM.inc(t.matmul(pS[:, 64:128], id64[:, :], Atst[:, :],
                                    start=False, stop=True))
                    MM.inc(t.matmul(pS[:, 128:256], Ntb[:, :],
                                    Zs[:, c * 128:c * 128 + 128],
                                    start=True, stop=False))
                    MM.inc(t.matmul(pS[:, 128:256], xv,
                                    Hts[:, c * 128:c * 128 + 128],
                                    start=False, stop=True))
                PE(_)
                SC_MM = MM.v

                def _(v, h=h, cg=cg, SC_MM=SC_MM):
                    v.wait_ge(s_mm, SC_MM)
                    DV.inc(v.tensor_copy(Ntst[:, :], pS[:, 0:64]))
                    DV.inc(v.tensor_copy(Atst[:, :], pS[:, 64:128]))
                    DV.inc(v.tensor_copy(
                        hslice(vf, h)[:, cg * 128:cg * 128 + 128],
                        pS[:, 128:256]))
                DVE(_)

        for h in range(H):
            def _(v, h=h):
                v.wait_ge(s_dv, DV.v)
                v.wait_ge(s_mm, MM.v)
                v.wait_ge(s_g, G.v)
                if h == 0:
                    v.wait_ge(s_dv, KHAT_DV)
                DV.inc(v.memset(Ntst[:, :], 0.0))
                DV.inc(v.tensor_copy(Atst[:, :], id64[:, :]))
            DVE(_)
            for w in range(4):
                wave(h, w)

            def _(t, h=h):
                t.wait_ge(s_dv, DV.v)
                MM.inc(t.transpose(pS[:, 256:320], Atst[:, :], id64[:, :]))
            PE(_)

            def _(v, h=h):
                v.wait_ge(s_mm, MM.v)
                DV.inc(v.tensor_copy(Ntb[:, :] if False else Atst[:, :],
                                     pS[:, 256:320]))
            DVE(_)

            def _(g, h=h):
                g.wait_ge(s_dv, DV.v)
                G.inc(g.dma_start(out=cbuf[64 * h:64 * h + 64, 0:64],
                                  in_=Atst[:, :]), 16)
                G.inc(g.dma_start(out=cbuf[64 * h:64 * h + 64, 64:128],
                                  in_=Ntst[:, :]), 16)
            GP(_)
        DN_DONE_DV = DV.v
        dump_dbg("y", v_fm, SEG, 0)
        if phases < 6:
            def _(s):
                s.wait_ge(s_dv, DV.v)
                s.wait_ge(s_ac, AC.v)
                s.wait_ge(s_w, W.v)
                s.wait_ge(s_g, G.v)
                s.wait_ge(s_wt, WT.v)
            SY(_)
            return nc

        pp = [pA, pB]
        marks6 = []
        idx = 0
        for oc in range(NJ):
            for (c0, cn) in TOK_TILES:
                p = pp[idx % 2]

                def _(t, oc=oc, c0=c0, cn=cn, p=p, i=idx):
                    if i < 2:
                        t.wait_ge(s_dv, DN_DONE_DV)
                        t.wait_ge(s_ac, AC.v)
                        t.wait_ge(s_wt, WO_RDY)
                    else:
                        t.wait_ge(s_dv, marks6[i - 2])
                    for kc in range(NJ):
                        MM.inc(t.matmul(
                            p[:, 0:cn],
                            wt[:, kc * 1024 + oc * 128:kc * 1024 + oc * 128 + 128],
                            vf(kc)[:, c0:c0 + cn],
                            start=(kc == 0), stop=(kc == NJ - 1)))
                PE(_)

                def _(v, oc=oc, c0=c0, cn=cn, p=p):
                    v.wait_ge(s_mm, MM.v)
                    DV.inc(v.tensor_copy(kf(oc)[:, c0:c0 + cn], p[:, 0:cn]))
                    marks6.append(DV.v)
                DVE(_)
                idx += 1
        WO_DONE = DV.v
        WR_RDY = wswap("wrT")

        for j in range(NJ):
            for (c0, cn) in TOK_TILES:
                def _(v, j=j, c0=c0, cn=cn):
                    v.wait_ge(s_dv, DV.v)
                    v.wait_ge(s_g, G.v)
                    xo = xnf(j)[:, 1 + c0:1 + c0 + cn]
                    xp = xnf(j)[:, c0:c0 + cn]
                    DV.inc(v.tensor_scalar_mul(scrb[:, 0:cn], xp,
                                               muc_sb[:, j:j + 1]))
                    v.wait_ge(s_dv, DV.v)
                    DV.inc(v.scalar_tensor_tensor(qf(j)[:, c0:c0 + cn], xo,
                                                  mu_sb[:, j:j + 1],
                                                  scrb[:, 0:cn],
                                                  ALU.mult, ALU.add))
                    v.wait_ge(s_dv, DV.v)
                    DV.inc(v.tensor_scalar_mul(scrb[:, 0:cn], xp,
                                               muc_sb[:, 8 + j:9 + j]))
                    v.wait_ge(s_dv, DV.v)
                    DV.inc(v.scalar_tensor_tensor(vf(j)[:, c0:c0 + cn], xo,
                                                  mu_sb[:, 8 + j:9 + j],
                                                  scrb[:, 0:cn],
                                                  ALU.mult, ALU.add))
                DVE(_)
        XRK_DV = DV.v

        pp = [pA, pB]
        marks7 = []
        idx = 0
        for oc in range(NJ):
            for (c0, cn) in TOK_TILES:
                p = pp[idx % 2]

                def _(t, oc=oc, c0=c0, cn=cn, p=p, i=idx):
                    if i < 2:
                        t.wait_ge(s_dv, XRK_DV)
                        t.wait_ge(s_wt, WR_RDY)
                    else:
                        t.wait_ge(s_ac, marks7[i - 2])
                    for kc in range(NJ):
                        MM.inc(t.matmul(
                            p[:, 0:cn],
                            wt[:, kc * 1024 + oc * 128:kc * 1024 + oc * 128 + 128],
                            qf(kc)[:, c0:c0 + cn],
                            start=(kc == 0), stop=(kc == NJ - 1)))
                PE(_)

                def _(sc, oc=oc, c0=c0, cn=cn, p=p):
                    sc.wait_ge(s_mm, MM.v)
                    AC.inc(sc.activation(xnf(oc)[:, c0:c0 + cn], p[:, 0:cn],
                                         AF.Sigmoid))
                    marks7.append(AC.v)
                ACT(_)
                idx += 1
        R_AC = AC.v
        WKK_RDY = wswap("wkkT")

        pp = [pA, pB]
        marks8 = []
        idx = 0
        for oc in range(NJ):
            for (c0, cn) in TOK_TILES:
                p = pp[idx % 2]

                def _(t, oc=oc, c0=c0, cn=cn, p=p, i=idx):
                    if i < 2:
                        t.wait_ge(s_ac, R_AC)
                        t.wait_ge(s_wt, WKK_RDY)
                    else:
                        t.wait_ge(s_dv, marks8[i - 2])
                    for kc in range(NJ):
                        MM.inc(t.matmul(
                            p[:, 0:cn],
                            wt[:, kc * 1024 + oc * 128:kc * 1024 + oc * 128 + 128],
                            vf(kc)[:, c0:c0 + cn],
                            start=(kc == 0), stop=(kc == NJ - 1)))
                PE(_)

                def _(v, oc=oc, c0=c0, cn=cn, p=p):
                    v.wait_ge(s_mm, MM.v)
                    v.wait_ge(s_dv, DV.v)
                    DV.inc(v.tensor_scalar_max(scrb[:, 0:cn], p[:, 0:cn], 0.0))
                    v.wait_ge(s_dv, DV.v)
                    DV.inc(v.tensor_mul(qf(oc)[:, c0:c0 + cn], scrb[:, 0:cn],
                                        scrb[:, 0:cn]))
                    marks8.append(DV.v)
                DVE(_)
                idx += 1
        KACT_DV = DV.v
        WVV_RDY = wswap("wvvT")

        pp = [pA, pB]
        marks9 = []
        idx = 0
        for oc in range(NJ):
            for (c0, cn) in TOK_TILES:
                p = pp[idx % 2]
                half = (idx % 2) * 512

                def _(s, oc=oc, c0=c0, cn=cn, i=idx, half=half):
                    if i >= 2:
                        s.wait_ge(s_dv, marks9[i - 2])
                    else:
                        s.wait_ge(s_dv, KACT_DV)
                    W.inc(s.dma_start(out=xld[:, half:half + cn],
                                      in_=xs[128 * oc:128 * oc + 128,
                                             1 + c0:1 + c0 + cn]), 16)
                SY(_)

                def _(t, oc=oc, c0=c0, cn=cn, p=p, i=idx):
                    if i < 2:
                        t.wait_ge(s_dv, KACT_DV)
                        t.wait_ge(s_wt, WVV_RDY)
                    else:
                        t.wait_ge(s_dv, marks9[i - 2])
                    for kc in range(NJ):
                        MM.inc(t.matmul(
                            p[:, 0:cn],
                            wt[:, kc * 1024 + oc * 128:kc * 1024 + oc * 128 + 128],
                            qf(kc)[:, c0:c0 + cn],
                            start=(kc == 0), stop=(kc == NJ - 1)))
                PE(_)

                def _(v, oc=oc, c0=c0, cn=cn, p=p, half=half):
                    v.wait_ge(s_mm, MM.v)
                    v.wait_ge(s_w, W.v)
                    v.wait_ge(s_dv, DV.v)
                    DV.inc(v.tensor_mul(scrb[:, 0:cn], p[:, 0:cn],
                                        xnf(oc)[:, c0:c0 + cn]))
                    DV.inc(v.tensor_add(scr2[:, half:half + cn],
                                        xld[:, half:half + cn],
                                        kf(oc)[:, c0:c0 + cn]))
                    v.wait_ge(s_dv, DV.v)
                    DV.inc(v.tensor_add(scr2[:, half:half + cn],
                                        scr2[:, half:half + cn],
                                        scrb[:, 0:cn]))
                DVE(_)

                def _(s, oc=oc, c0=c0, cn=cn, half=half):
                    s.wait_ge(s_dv, DV.v)
                    W.inc(s.dma_start(out=x1p[128 * oc:128 * oc + 128,
                                              c0:c0 + cn],
                                      in_=scr2[:, half:half + cn]), 16)
                    marks9.append(DV.v)
                SY(_)
                idx += 1

        def _(s):
            s.wait_ge(s_w, W.v)
            s.wait_ge(s_g, G.v)
        SY(_)

    return nc



def build_launch_B(debug=()):
    nc = bass.Bass("TRN2", target_bir_lowering=False)

    x1p_in = nc.dram_tensor("x1p_in", [C_, SEG], F32, kind="ExternalInput").ap()
    zbuf_in = nc.dram_tensor("zbuf_in", [C_, SEG], BF16, kind="ExternalInput").ap()
    abuf_in = nc.dram_tensor("abuf_in", [C_, NCH * 64], F32, kind="ExternalInput").ap()
    comps_in = nc.dram_tensor("comps_in", [4 * C_, 128], F32, kind="ExternalInput").ap()
    flags_in = nc.dram_tensor("flags_in", [1, 4], F32, kind="ExternalInput").ap()
    w_in = {}
    for n in ["woT", "wq1T", "wk1T", "wv1T"]:
        w_in[n] = nc.dram_tensor(n, [C_, C_], F32, kind="ExternalInput").ap()
    gwT_in = nc.dram_tensor("gwT", [C_, 16], F32, kind="ExternalInput").ap()
    bias_in = nc.dram_tensor("bias_in", [128, 24], F32, kind="ExternalInput").ap()
    rope2_in = nc.dram_tensor("rope2", [128, SEG], BF16, kind="ExternalInput").ap()
    ebias_in = nc.dram_tensor("ebias_bc", [128, 16], F32, kind="ExternalInput").ap()
    id64x2_in = nc.dram_tensor("id64x2", [128, 64], F32, kind="ExternalInput").ap()
    id128f_in = nc.dram_tensor("id128f", [128, 128], F32, kind="ExternalInput").ap()
    ones_in = nc.dram_tensor("ones_in", [128, 128], F32, kind="ExternalInput").ap()

    x1o = nc.dram_tensor("x1o", [C_, SEG], F32, kind="ExternalOutput").ap()
    xn1o = nc.dram_tensor("xn1o", [C_, SEG], BF16, kind="ExternalOutput").ap()
    q1o = nc.dram_tensor("q1o", [C_, SEG], BF16, kind="ExternalOutput").ap()
    k1o = nc.dram_tensor("k1o", [C_, SEG], BF16, kind="ExternalOutput").ap()
    v1o = nc.dram_tensor("v1o", [C_, SEG], BF16, kind="ExternalOutput").ap()
    combo = nc.dram_tensor("combo", [16, SEG], F32, kind="ExternalOutput").ap()
    dbg_out = {}
    for name in debug:
        dbg_out[name] = nc.dram_tensor("dbg_" + name, [C_, SEG], F32,
                                       kind="ExternalOutput").ap()

    from contextlib import ExitStack
    with ExitStack() as _stk:
        E = _stk.enter_context
        xbig = E(nc.sbuf_tensor("xbig", [128, NJ * SEG], F32))
        x1 = xbig
        absb = xbig[:, 0:NJ * 1024]
        cmsb = xbig[:, NJ * 1024:NJ * 1024 + 4 * NJ * 128]
        zb = E(nc.sbuf_tensor("zb", [128, NJ * SEG], BF16))
        ycorr = E(nc.sbuf_tensor("ycorr", [128, NJ * SEG], BF16))
        wt = E(nc.sbuf_tensor("wt", [128, NJ * 1024], BF16))
        rows = E(nc.sbuf_tensor("rows", [1, 2560], F32))
        sq16 = E(nc.sbuf_tensor("sq16", [128, 512], BF16))
        consts = E(nc.sbuf_tensor("consts", [128, 16], BF16))
        ones1f = E(nc.sbuf_tensor("ones1f", [1, 128], F32))
        onescf = E(nc.sbuf_tensor("onescf", [128, 1], F32))
        onescb = E(nc.sbuf_tensor("onescb", [128, 1], BF16))
        id64x2 = E(nc.sbuf_tensor("id64x2_sb", [128, 64], F32))
        id128f = E(nc.sbuf_tensor("id128f_sb", [128, 128], F32))
        flagr = E(nc.sbuf_tensor("flagr", [1, 4], F32))
        flagbc = E(nc.sbuf_tensor("flagbc", [128, 8], F32))
        Sbuf = E(nc.sbuf_tensor("Sbuf", [128, 64], F32))
        Sb16 = E(nc.sbuf_tensor("Sb16", [128, 64], BF16))
        Atmp = E(nc.sbuf_tensor("Atmp", [128, 128], F32))
        bias_sb = E(nc.sbuf_tensor("bias_sb", [128, 24], F32))
        rope2 = E(nc.sbuf_tensor("rope2_sb", [128, SEG], BF16))
        ebias = E(nc.sbuf_tensor("ebias_sb", [128, 16], F32))
        gw = E(nc.sbuf_tensor("gw_sb", [128, NJ * 16], BF16))
        gates = E(nc.sbuf_tensor("gates", [16, SEG], F32))
        epsc = E(nc.sbuf_tensor("epsc", [1, 1], F32))
        tkm = E(nc.sbuf_tensor("tkm", [128, 16], F32))
        cmp16 = E(nc.sbuf_tensor("cmp16", [128, 256], F32))
        cnt = E(nc.sbuf_tensor("cnt", [128, 16], F32))
        comb = E(nc.sbuf_tensor("comb", [128, 16], F32))
        dnm = E(nc.sbuf_tensor("dnm", [128, 1], F32))
        pA = E(nc.psum_tensor("pA", [128, 512], F32))
        pB = E(nc.psum_tensor("pB", [128, 512], F32))
        pS = E(nc.psum_tensor("pS", [64, 512], F32))
        pR = E(nc.psum_tensor("pR", [128, 512], F32))
        pT = E(nc.psum_tensor("pT", [128, 512], F32))
        s_g = E(nc.semaphore("s_g"))
        s_w = E(nc.semaphore("s_w"))
        s_wt = E(nc.semaphore("s_wt"))
        s_mm = E(nc.semaphore("s_mm"))
        s_dv = E(nc.semaphore("s_dv"))
        s_ac = E(nc.semaphore("s_ac"))
        block = E(nc.Block())
        G = Ctr(s_g)
        W = Ctr(s_w)
        WT = Ctr(s_wt)
        MM = Ctr(s_mm)
        DV = Ctr(s_dv)
        AC = Ctr(s_ac)

        PE = block.tensor
        DVE = block.vector
        ACT = block.scalar
        GP = block.gpsimd
        SY = block.sync

        def x1f(j):
            return x1[:, j * SEG:(j + 1) * SEG]

        def zbf(j):
            return zb[:, j * SEG:(j + 1) * SEG]

        def ycf(j):
            return ycorr[:, j * SEG:(j + 1) * SEG]

        def hsl(buf_f, h):
            r0 = 64 * (h % 2)
            return buf_f(h // 2)[r0:r0 + 64, :]

        def ahsl(h, c):
            r0 = 64 * (h % 2)
            j = h // 2
            return absb[r0:r0 + 64, j * 1024 + 64 * c:j * 1024 + 64 * c + 64]

        def cmsl(r, h, part):
            r0 = 64 * (h % 2)
            j = h // 2
            base = (r * NJ + j) * 128 + 64 * part
            return cmsb[r0:r0 + 64, base:base + 64]

        def wload(name):
            return w_in[name].rearrange("(kc p) o -> p kc o", p=128)

        def _(s):
            W.inc(s.dma_start(out=zb[:, :],
                              in_=zbuf_in.rearrange("(j p) t -> p j t", p=128)), 16)
            W.inc(s.dma_start(out=absb.rearrange("p (j t) -> p j t", j=NJ),
                              in_=abuf_in.rearrange("(j p) t -> p j t", p=128)), 16)
            W.inc(s.dma_start(out=cmsb.rearrange("p (r j t) -> p r j t", r=4, j=NJ),
                              in_=comps_in.rearrange("(r j p) t -> p r j t", p=128, j=NJ)), 16)
        SY(_)

        def _(g):
            G.inc(g.dma_start(out=flagr[:, :], in_=flags_in), 16)
            G.inc(g.dma_start(out=id64x2[:, :], in_=id64x2_in), 16)
            G.inc(g.dma_start(out=id128f[:, :], in_=id128f_in), 16)
            G.inc(g.dma_start(out=ones1f[:, :], in_=ones_in[0:1, :]), 16)
            G.inc(g.dma_start(out=onescf[:, :], in_=ones_in[:, 0:1], allow_slow_non_contiguous=True), 16)
            G.inc(g.dma_start(out=onescb[:, :], in_=ones_in[:, 1:2], allow_slow_non_contiguous=True), 16)
            G.inc(g.dma_start(out=bias_sb[:, :], in_=bias_in), 16)
            G.inc(g.dma_start(out=rope2[:, :], in_=rope2_in), 16)
            G.inc(g.dma_start(out=ebias[:, :], in_=ebias_in), 16)
            G.inc(g.dma_start(out=gw.ap().rearrange("p (kc o) -> p kc o", kc=NJ),
                              in_=gwT_in.rearrange("(kc p) o -> p kc o", p=128)), 16)
            WT.inc(g.dma_start(out=wt.ap().rearrange("p (kc o) -> p kc o", kc=NJ),
                               in_=wload("woT")), 16)
        GP(_)
        LOADED_G = G.v

        def _(v):
            v.wait_ge(s_g, LOADED_G)
            DV.inc(v.memset(epsc[:, :], EPS))
        DVE(_)

        def _(t):
            t.wait_ge(s_g, LOADED_G)
            MM.inc(t.matmul(pR[:, 0:4], ones1f[:, :], flagr[:, :],
                            start=True, stop=True))
        PE(_)

        def _(v):
            v.wait_ge(s_mm, MM.v)
            DV.inc(v.tensor_copy(flagbc[:, 0:4], pR[:, 0:4]))
            v.wait_ge(s_dv, DV.v)
            DV.inc(v.tensor_scalar(flagbc[:, 4:8], flagbc[:, 0:4], -1.0, 1.0,
                                   ALU.mult, ALU.add))
        DVE(_)
        PREP_DV = DV.v

        for h in range(H):
            r0 = 64 * (h % 2)
            Ssl = Sbuf[r0:r0 + 64, :]
            Sbsl = Sb16[r0:r0 + 64, :]
            idp = id64x2[r0:r0 + 64, :]
            Asl = Atmp[r0:r0 + 64, 0:64]
            Bsl = Atmp[r0:r0 + 64, 64:128]

            def _(v, h=h):
                v.wait_ge(s_dv, DV.v)
                v.wait_ge(s_mm, MM.v)
                if h == 0:
                    v.wait_ge(s_w, 48)
                    v.wait_ge(s_dv, PREP_DV)
                DV.inc(v.memset(Sbuf[r0:r0 + 64, :] if False else Ssl, 0.0))
            DVE(_)

            for r in range(4):
                def _(v, h=h, r=r):
                    v.wait_ge(s_dv, DV.v)
                    v.wait_ge(s_mm, MM.v)
                    DV.inc(v.tensor_scalar_mul(Asl, idp,
                                               flagbc[r0:r0 + 64, 4 + r:5 + r]))
                    v.wait_ge(s_dv, DV.v)
                    DV.inc(v.scalar_tensor_tensor(Asl, cmsl(r, h, 0),
                                                  flagbc[r0:r0 + 64, r:r + 1],
                                                  Asl, ALU.mult, ALU.add))
                    DV.inc(v.tensor_scalar_mul(Bsl, cmsl(r, h, 1),
                                               flagbc[r0:r0 + 64, r:r + 1]))
                DVE(_)
                AB_DV = DV.v

                def _(t, h=h, r=r, AB_DV=AB_DV):
                    t.wait_ge(s_dv, AB_DV)
                    MM.inc(t.matmul(pS[:, 0:64], Asl, Ssl, start=True, stop=False))
                    MM.inc(t.matmul(pS[:, 0:64], idp, Bsl, start=False, stop=True))
                PE(_)

                def _(v, h=h, r=r):
                    v.wait_ge(s_mm, MM.v)
                    DV.inc(v.tensor_copy(Ssl, pS[:, 0:64]))
                DVE(_)

            for c in range(NCH):
                def _(v, h=h):
                    v.wait_ge(s_dv, DV.v)
                    v.wait_ge(s_mm, MM.v)
                    DV.inc(v.tensor_copy(Sbsl, Ssl))
                DVE(_)
                SB_DV = DV.v

                def _(t, h=h, c=c, SB_DV=SB_DV):
                    t.wait_ge(s_dv, SB_DV)
                    MM.inc(t.matmul(pS[:, 64:192], Sbsl,
                                    hsl(zbf, h)[:, c * 128:c * 128 + 128],
                                    start=True, stop=True))
                    MM.inc(t.matmul(pS[:, 192:256], ahsl(h, c), Ssl,
                                    start=True, stop=False))
                    MM.inc(t.matmul(pS[:, 192:256], idp, Ssl,
                                    start=False, stop=True))
                PE(_)
                SC_MM = MM.v

                def _(v, h=h, c=c, SC_MM=SC_MM):
                    v.wait_ge(s_mm, SC_MM)
                    DV.inc(v.tensor_copy(
                        hsl(ycf, h)[:, c * 128:c * 128 + 128], pS[:, 64:192]))
                    DV.inc(v.tensor_copy(Ssl, pS[:, 192:256]))
                DVE(_)
        PHB_DV = DV.v

        for name in debug:
            if name != "yc":
                continue

            def _(g):
                g.wait_ge(s_dv, PHB_DV)
                G.inc(g.dma_start(
                    out=dbg_out["yc"].rearrange("(j p) t -> p j t", p=128),
                    in_=ycorr[:, :].rearrange("p (j t) -> p j t", j=NJ)), 16)
            GP(_)
        YC_DUMP_G = G.v

        def _(s):
            s.wait_ge(s_dv, PHB_DV)
            s.wait_ge(s_mm, MM.v)
            W.inc(s.dma_start(out=x1[:, :],
                              in_=x1p_in.rearrange("(j p) t -> p j t", p=128)), 16)
        SY(_)
        X1_W = W.v

        pp = [pA, pB]
        marks = []
        idx = 0
        for oc in range(NJ):
            for (c0, cn) in TOK_TILES:
                p = pp[idx % 2]

                def _(t, oc=oc, c0=c0, cn=cn, p=p, i=idx):
                    if i < 2:
                        t.wait_ge(s_dv, PHB_DV)
                        t.wait_ge(s_wt, 16)
                        t.wait_ge(s_w, X1_W)
                    else:
                        t.wait_ge(s_dv, marks[i - 2])
                    for kc in range(NJ):
                        MM.inc(t.matmul(
                            p[:, 0:cn],
                            wt[:, kc * 1024 + oc * 128:kc * 1024 + oc * 128 + 128],
                            ycf(kc)[:, c0:c0 + cn],
                            start=(kc == 0), stop=(kc == NJ - 1)))
                PE(_)

                def _(v, oc=oc, c0=c0, cn=cn, p=p):
                    v.wait_ge(s_mm, MM.v)
                    xo = x1f(oc)[:, c0:c0 + cn]
                    DV.inc(v.tensor_add(xo, xo, p[:, 0:cn]))
                    marks.append(DV.v)
                DVE(_)
                idx += 1
        X1_DV = DV.v

        def wswap(name):
            def _(g, name=name):
                g.wait_ge(s_mm, MM.v)
                WT.inc(g.dma_start(out=wt.ap().rearrange("p (kc o) -> p kc o", kc=NJ),
                                   in_=wload(name)), 16)
            GP(_)
            return WT.v

        WQ1_RDY = wswap("wq1T")

        r_sum = rows[:, 0:512]
        r_ssq = rows[:, 512:1024]
        r_negm = rows[:, 1024:1536]
        r_rstd = rows[:, 1536:2048]
        r_tmp = rows[:, 2048:2560]
        for ti, (c0, cn) in enumerate(TOK_TILES):
            def _(t, c0=c0, cn=cn, ti=ti):
                t.wait_ge(s_dv, DV.v)
                if ti == 0:
                    t.wait_ge(s_dv, X1_DV)
                for j in range(NJ):
                    MM.inc(t.matmul(pT[0:1, 0:cn], onescf[:, :],
                                    x1f(j)[:, c0:c0 + cn],
                                    start=(j == 0), stop=(j == NJ - 1)))
            PE(_)
            for j in range(NJ):
                def _(sc, j=j, c0=c0, cn=cn):
                    sc.wait_ge(s_mm, MM.v)
                    AC.inc(sc.activation(sq16[:, 0:cn], x1f(j)[:, c0:c0 + cn],
                                         AF.Square))
                ACT(_)

                def _(t, j=j, cn=cn):
                    t.wait_ge(s_ac, AC.v)
                    MM.inc(t.matmul(pR[0:1, 0:cn], onescb[:, :],
                                    sq16[:, 0:cn],
                                    start=(j == 0), stop=(j == NJ - 1)))
                PE(_)

            def _(v, c0=c0, cn=cn):
                v.wait_ge(s_mm, MM.v)
                DV.inc(v.tensor_copy(r_sum[:, 0:cn], pT[0:1, 0:cn]))
                DV.inc(v.tensor_copy(r_ssq[:, 0:cn], pR[0:1, 0:cn]))
                v.wait_ge(s_dv, DV.v)
                DV.inc(v.tensor_scalar_mul(r_negm[:, 0:cn], r_sum[:, 0:cn],
                                           -1.0 / C_))
                v.wait_ge(s_dv, DV.v)
                DV.inc(v.tensor_mul(r_tmp[:, 0:cn], r_negm[:, 0:cn],
                                    r_negm[:, 0:cn]))
                DV.inc(v.tensor_scalar(r_rstd[:, 0:cn], r_ssq[:, 0:cn],
                                       1.0 / C_, None, ALU.mult, ALU.bypass))
                v.wait_ge(s_dv, DV.v)
                DV.inc(v.tensor_sub(r_rstd[:, 0:cn], r_rstd[:, 0:cn],
                                    r_tmp[:, 0:cn]))
            DVE(_)

            def _(sc, cn=cn):
                sc.wait_ge(s_dv, DV.v)
                AC.inc(sc.activation(r_tmp[:, 0:cn], r_rstd[:, 0:cn], AF.Sqrt,
                                     bias=epsc[:, :]))
            ACT(_)

            def _(v, cn=cn):
                v.wait_ge(s_ac, AC.v)
                DV.inc(v.reciprocal(r_rstd[:, 0:cn], r_tmp[:, 0:cn]))
            DVE(_)

            def _(t, cn=cn):
                t.wait_ge(s_dv, DV.v)
                MM.inc(t.matmul(pA[:, 0:cn], ones1f[:, :], r_negm[:, 0:cn],
                                start=True, stop=True))
                MM.inc(t.matmul(pB[:, 0:cn], ones1f[:, :], r_rstd[:, 0:cn],
                                start=True, stop=True))
            PE(_)

            def _(v, c0=c0, cn=cn):
                v.wait_ge(s_mm, MM.v)
                for j in range(NJ):
                    DV.inc(v.tensor_add(zbf(j)[:, c0:c0 + cn],
                                        x1f(j)[:, c0:c0 + cn], pA[:, 0:cn]))
                v.wait_ge(s_dv, DV.v)
                for j in range(NJ):
                    DV.inc(v.tensor_mul(zbf(j)[:, c0:c0 + cn],
                                        zbf(j)[:, c0:c0 + cn], pB[:, 0:cn]))
            DVE(_)
        LN1_DV = DV.v
        xn1f = zbf

        def _(s):
            s.wait_ge(s_dv, LN1_DV)
            W.inc(s.dma_start(out=x1o.rearrange("(j p) t -> p j t", p=128),
                              in_=x1[:, :].rearrange("p (j t) -> p j t", j=NJ)), 16)
            W.inc(s.dma_start(out=xn1o.rearrange("(j p) t -> p j t", p=128),
                              in_=zb[:, :].rearrange("p (j t) -> p j t", j=NJ)), 16)
        SY(_)

        stg = ycorr

        def emit_proj1(wname_rdy, evac_kind, out_dram, buf_f=None):
            pp = [pA, pB]
            marks2 = []
            entry_dv = DV.v
            idx2 = 0
            for oc in range(NJ):
                for (c0, cn) in TOK_TILES:
                    p = pp[idx2 % 2]

                    def _(t, oc=oc, c0=c0, cn=cn, p=p, i=idx2):
                        if i < 2:
                            t.wait_ge(s_wt, wname_rdy)
                            t.wait_ge(s_dv, entry_dv)
                            t.wait_ge(s_w, W.v)
                        else:
                            t.wait_ge(s_dv, marks2[i - 2])
                        for kc in range(NJ):
                            MM.inc(t.matmul(
                                p[:, 0:cn],
                                wt[:, kc * 1024 + oc * 128:kc * 1024 + oc * 128 + 128],
                                xn1f(kc)[:, c0:c0 + cn],
                                start=(kc == 0), stop=(kc == NJ - 1)))
                    PE(_)

                    def _(v, oc=oc, c0=c0, cn=cn, p=p):
                        v.wait_ge(s_mm, MM.v)
                        v.wait_ge(s_g, G.v)
                        bcolv = bias_sb[:, {"q": 0, "k": 8, "v": 16}[evac_kind] + oc:
                                        {"q": 0, "k": 8, "v": 16}[evac_kind] + oc + 1]
                        if evac_kind in ("q", "k"):
                            DV.inc(v.scalar_tensor_tensor(
                                buf_f(oc)[:, c0:c0 + cn], p[:, 0:cn], bcolv,
                                rope2[:, c0:c0 + cn], ALU.add, ALU.mult))
                        else:
                            DV.inc(v.tensor_scalar(buf_f(oc)[:, c0:c0 + cn],
                                                   p[:, 0:cn], bcolv, None,
                                                   ALU.add, ALU.bypass))
                        marks2.append(DV.v)
                    DVE(_)
                    idx2 += 1
            def _(s2):
                s2.wait_ge(s_dv, DV.v)
                W.inc(s2.dma_start(out=out_dram.rearrange("(j p) t -> p j t", p=128),
                                   in_=buf_f(None) if False else None), 16)
            return

        def qbuf(j):
            return ycorr[:, j * SEG:(j + 1) * SEG]
        emit_proj1(WQ1_RDY, "q", q1o, qbuf)
        def _(s):
            s.wait_ge(s_dv, DV.v)
            W.inc(s.dma_start(out=q1o.rearrange("(j p) t -> p j t", p=128),
                              in_=ycorr[:, :].rearrange("p (j t) -> p j t", j=NJ)), 16)
        SY(_)
        WK1_RDY = wswap("wk1T")
        emit_proj1(WK1_RDY, "k", k1o, qbuf)
        def _(s):
            s.wait_ge(s_dv, DV.v)
            W.inc(s.dma_start(out=k1o.rearrange("(j p) t -> p j t", p=128),
                              in_=ycorr[:, :].rearrange("p (j t) -> p j t", j=NJ)), 16)
        SY(_)
        WV1_RDY = wswap("wv1T")
        emit_proj1(WV1_RDY, "v", v1o, qbuf)
        def _(s):
            s.wait_ge(s_dv, DV.v)
            W.inc(s.dma_start(out=v1o.rearrange("(j p) t -> p j t", p=128),
                              in_=ycorr[:, :].rearrange("p (j t) -> p j t", j=NJ)), 16)
        SY(_)
        PROJ_W = W.v

        for tt, (c0, cn) in enumerate(TOK_TILES):
            def _(t, c0=c0, cn=cn, tt=tt):
                t.wait_ge(s_dv, DV.v)
                if tt == 0:
                    t.wait_ge(s_g, LOADED_G)
                for kc in range(NJ):
                    MM.inc(t.matmul(pS[0:16, 0:cn],
                                    gw[:, kc * 16:kc * 16 + 16],
                                    xn1f(kc)[:, c0:c0 + cn],
                                    start=(kc == 0), stop=(kc == NJ - 1)))
            PE(_)

            def _(v, c0=c0, cn=cn):
                v.wait_ge(s_mm, MM.v)
                DV.inc(v.tensor_copy(gates[:, c0:c0 + cn], pS[0:16, 0:cn]))
            DVE(_)
        GATES_DV = DV.v

        for ti in range(16):
            def _(t, ti=ti):
                t.wait_ge(s_dv, DV.v)
                MM.inc(t.transpose(pT[:, 0:16], gates[:, ti * 128:ti * 128 + 128],
                                   id128f[0:16, 0:16]))
            PE(_)
            TR_MM = MM.v

            def _(sc, ti=ti, TR_MM=TR_MM):
                sc.wait_ge(s_mm, TR_MM)
                AC.inc(sc.activation(tkm[:, :], pT[:, 0:16], AF.Sigmoid))
            ACT(_)
            SIG_AC = AC.v

            def _(v, ti=ti, SIG_AC=SIG_AC):
                v.wait_ge(s_ac, SIG_AC)
                DV.inc(v.tensor_add(cnt[:, 0:16], tkm[:, :], ebias[:, :]))
                v.wait_ge(s_dv, DV.v)
                for f in range(16):
                    DV.inc(v.tensor_tensor(
                        bass.AP(cmp16, f, [[256, 128], [16, 16]]),
                        cnt[:, 0:16],
                        cnt[:, f:f + 1].broadcast_to([128, 16]), ALU.is_lt))
                v.wait_ge(s_dv, DV.v)
                DV.inc(v.tensor_reduce(comb[:, 0:1].broadcast_to([128, 1]) if False else dnm[:, :],
                                       cmp16[:, :], AX, ALU.add) if False else
                       v.tensor_reduce(cnt[:, 0:16],
                                       cmp16.ap().rearrange("p (e f) -> p e f", e=16),
                                       AX, ALU.add))
                v.wait_ge(s_dv, DV.v)
                DV.inc(v.tensor_scalar(cnt[:, 0:16], cnt[:, 0:16], float(TOPK),
                                       None, ALU.is_lt, ALU.bypass))
                v.wait_ge(s_dv, DV.v)
                DV.inc(v.tensor_mul(comb[:, :], tkm[:, :], cnt[:, 0:16]))
                v.wait_ge(s_dv, DV.v)
                DV.inc(v.tensor_reduce(dnm[:, :], comb[:, :], AX, ALU.add))
                v.wait_ge(s_dv, DV.v)
                DV.inc(v.reciprocal(dnm[:, :], dnm[:, :]))
                v.wait_ge(s_dv, DV.v)
                DV.inc(v.tensor_scalar(comb[:, :], comb[:, :], dnm[:, :],
                                       ROUTE_SCALE, ALU.mult, ALU.mult))
            DVE(_)
            CMB_DV = DV.v

            def _(t, ti=ti, CMB_DV=CMB_DV):
                t.wait_ge(s_dv, CMB_DV)
                MM.inc(t.transpose(pT[0:16, 128:256], comb[:, :], id128f[:, :]))
            PE(_)

            def _(v, ti=ti):
                v.wait_ge(s_mm, MM.v)
                DV.inc(v.tensor_copy(gates[:, ti * 128:ti * 128 + 128],
                                     pT[0:16, 128:256]))
            DVE(_)
        COMB_DV = DV.v

        def _(s):
            s.wait_ge(s_dv, COMB_DV)
            W.inc(s.dma_start(out=combo, in_=gates[:, :]), 16)
        SY(_)

        def _(s):
            s.wait_ge(s_w, W.v)
            s.wait_ge(s_g, G.v)
            s.wait_ge(s_wt, WT.v)
        SY(_)

    return nc


def _shard_inputs_A(ins, core):
    b, seg = core // 4, core % 4
    x = np.asarray(ins["L0_attn_x"])
    t0 = seg * SEG
    xsh = np.zeros((SEGP, C_), np.float32)
    xsh[1:, :] = x[b, t0:t0 + SEG, :]
    if t0 > 0:
        xsh[0, :] = x[b, t0 - 1, :]
    cc = host_consts()
    mu = np.zeros((128, 16), np.float32)
    mu[:, 0:8] = np.asarray(ins["L0_ffn_mu_r"]).reshape(8, 128).T
    mu[:, 8:16] = np.asarray(ins["L0_ffn_mu_k"]).reshape(8, 128).T
    d = {
        "xs": np.ascontiguousarray(xsh.T),
        "wq": np.asarray(ins["L0_attn_Wq"]),
        "wk": np.asarray(ins["L0_attn_Wk"]),
        "wv": np.asarray(ins["L0_attn_Wv"]),
        "woT": np.asarray(ins["L0_attn_Wo"]),
        "wrT": np.ascontiguousarray(np.asarray(ins["L0_ffn_W_r"]).T),
        "wkkT": np.ascontiguousarray(np.asarray(ins["L0_ffn_W_k"]).T),
        "wvvT": np.ascontiguousarray(np.asarray(ins["L0_ffn_W_v"]).T),
        "ab_raw": np.concatenate([np.asarray(ins["L0_attn_alpha_raw"]).reshape(16),
                                  np.asarray(ins["L0_attn_beta_raw"]).reshape(16)]).reshape(1, 32),
        "mu_in": mu,
        "stril4": cc["stril4"], "striu4": cc["striu4"], "striuS4": cc["striuS4"],
        "identb": cc["identb"], "id64b": cc["id64b"], "id64_in": cc["id64_in"],
        "ones_in": cc["ones_in"], "e16": cc["e16"], "e2sel": cc["e2sel"],
    }
    return {k: np.ascontiguousarray(v) for k, v in d.items()}



def build_launch_C(debug=(), compiled=False):
    nc = Bacc("TRN2", target_bir_lowering=False)

    x1c = nc.dram_tensor("x1c", [C_, SEG], F32, kind="ExternalInput").ap()
    xn1c = nc.dram_tensor("xn1c", [C_, SEG], BF16, kind="ExternalInput").ap()
    q1c = nc.dram_tensor("q1c", [C_, SEG], BF16, kind="ExternalInput").ap()
    Kf_in = nc.dram_tensor("Kf", [C_, T_], BF16, kind="ExternalInput").ap()
    vhead_in = nc.dram_tensor("vhead", [16 * 128, 4096], BF16, kind="ExternalInput").ap()
    cmb_in = nc.dram_tensor("cmb", [16, SEG], F32, kind="ExternalInput").ap()
    wg_in = nc.dram_tensor("wgT3", [NEXP * C_, C_], F32, kind="ExternalInput").ap()
    wu_in = nc.dram_tensor("wuT3", [NEXP * C_, C_], F32, kind="ExternalInput").ap()
    wd_in = nc.dram_tensor("wdT3", [NEXP * C_, C_], F32, kind="ExternalInput").ap()
    shg_in = nc.dram_tensor("shgT", [C_, C_], F32, kind="ExternalInput").ap()
    shu_in = nc.dram_tensor("shuT", [C_, C_], F32, kind="ExternalInput").ap()
    shd_in = nc.dram_tensor("shdT", [C_, C_], F32, kind="ExternalInput").ap()
    wo1_in = nc.dram_tensor("wo1T", [C_, C_], F32, kind="ExternalInput").ap()
    bo1_in = nc.dram_tensor("bo1c", [128, 8], F32, kind="ExternalInput").ap()
    tri01_in = nc.dram_tensor("tri01", [128, 128], BF16, kind="ExternalInput").ap()
    ob128_in = nc.dram_tensor("ob128", [128, 128], BF16, kind="ExternalInput").ap()
    ones_in = nc.dram_tensor("ones_in", [128, 128], F32, kind="ExternalInput").ap()
    e16_in = nc.dram_tensor("e16", [128, 256], BF16, kind="ExternalInput").ap()
    mrow_in = nc.dram_tensor("mrow", [1, 256], F32, kind="ExternalInput").ap()
    fixoff_in = nc.dram_tensor("fixoff", [1, 40], I32, kind="ExternalInput").ap()
    id128f_in = nc.dram_tensor("id128f", [128, 128], F32, kind="ExternalInput").ap()
    e16sel_in = nc.dram_tensor("e16sel", [16, 2048], BF16, kind="ExternalInput").ap()

    ybspill = nc.dram_tensor("ybspill", [128, NJ * SEG], BF16).ap()

    outp = nc.dram_tensor("outp", [C_, SEG], F32, kind="ExternalOutput").ap()
    dbg_out = {}
    for name in debug:
        dbg_out[name] = nc.dram_tensor("dbg_" + name, [C_, SEG], F32,
                                       kind="ExternalOutput").ap()

    from contextlib import ExitStack
    with ExitStack() as _stk:
        E = _stk.enter_context
        ksqt = E(nc.sbuf_tensor("ksqt", [128, 512], BF16))
        nscr = E(nc.sbuf_tensor("nscr", [16, 512], F32))
        nmax = E(nc.sbuf_tensor("nmax", [16, 2], F32))
        negU = E(nc.sbuf_tensor("negU", [128, 32], F32))
        lrec = E(nc.sbuf_tensor("lrec", [128, 512], BF16))
        lrf = E(nc.sbuf_tensor("lrf", [128, 512], F32))
        rb = E(nc.sbuf_tensor("rb", [64, 512], BF16))
        ystg = E(nc.sbuf_tensor("ystg", [64, 512], BF16))
        tri01 = E(nc.sbuf_tensor("tri01_sb", [128, 128], BF16))
        ob128 = E(nc.sbuf_tensor("ob128_sb", [128, 128], BF16))
        ones1f = E(nc.sbuf_tensor("ones1f", [1, 128], F32))
        e16 = E(nc.sbuf_tensor("e16_sb", [128, 256], BF16))
        mrow = E(nc.sbuf_tensor("mrow_sb", [1, 256], F32))
        mtbB = E(nc.sbuf_tensor("mtbB", [128, 256], F32))
        mtb = E(nc.sbuf_tensor("mtb", [128, 256], F32))
        fixoff = E(nc.sbuf_tensor("fixoff_sb", [1, 40], I32))
        bo1 = E(nc.sbuf_tensor("bo1_sb", [128, 8], F32))
        cmbb = E(nc.sbuf_tensor("cmbb", [16, SEG], BF16))
        idf = E(nc.sbuf_tensor("idf_sb", [128, 128], F32))
        onesf = E(nc.sbuf_tensor("onesf_sb", [128, 128], F32))
        e16sel = E(nc.sbuf_tensor("e16sel_sb", [16, 2048], BF16))
        pA = E(nc.psum_tensor("pA", [128, 512], F32))
        pB = E(nc.psum_tensor("pB", [128, 512], F32))
        pC = E(nc.psum_tensor("pC", [128, 512], F32))
        pD = E(nc.psum_tensor("pD", [128, 512], F32))
        pY = E(nc.psum_tensor("pY", [65, 512], F32))
        pZ = E(nc.psum_tensor("pZ", [64, 512], F32))
        pS = E(nc.psum_tensor("pS", [64, 512], F32))
        pR = E(nc.psum_tensor("pR", [128, 512], F32))
        s_g = E(nc.semaphore("s_g"))
        s_w = E(nc.semaphore("s_w"))
        s_wt = E(nc.semaphore("s_wt"))
        s_kv = E(nc.semaphore("s_kv"))
        s_mm = E(nc.semaphore("s_mm"))
        s_dv = E(nc.semaphore("s_dv"))
        s_ac = E(nc.semaphore("s_ac"))
        block = E(nc.Block())
        G = Ctr(s_g)
        W = Ctr(s_w)
        WT = Ctr(s_wt)
        KV = Ctr(s_kv)
        MM = Ctr(s_mm)
        DV = Ctr(s_dv)
        AC = Ctr(s_ac)

        PE = block.tensor
        DVE = block.vector
        ACT = block.scalar
        GP = block.gpsimd
        SY = block.sync

        def _(g):
            G.inc(g.dma_start(out=tri01[:, :], in_=tri01_in), 16)
            G.inc(g.dma_start(out=ob128[:, :], in_=ob128_in), 16)
            G.inc(g.dma_start(out=ones1f[:, :], in_=ones_in[0:1, :]), 16)
            G.inc(g.dma_start(out=e16[:, :], in_=e16_in), 16)
            G.inc(g.dma_start(out=mrow[:, :], in_=mrow_in), 16)
            G.inc(g.dma_start(out=fixoff[:, :], in_=fixoff_in), 16)
            G.inc(g.dma_start(out=bo1[:, :], in_=bo1_in), 16)
            G.inc(g.dma_start(out=cmbb[:, :], in_=cmb_in), 16)
            G.inc(g.dma_start(out=idf[:, :], in_=id128f_in), 16)
            G.inc(g.dma_start(out=onesf[:, :], in_=ones_in), 16)
            G.inc(g.dma_start(out=e16sel[:, :], in_=e16sel_in), 16)
        GP(_)
        LOADED_G = G.v

        with ExitStack() as att:
            A_ = att.enter_context
            Pb = A_(nc.sbuf_tensor("Pb", [128, 64 * 512], BF16))
            q1 = A_(nc.sbuf_tensor("q1", [128, NJ * SEG], BF16))
            ybuf = A_(nc.sbuf_tensor("ybuf", [128, NJ * SEG], BF16))
            khb = A_(nc.sbuf_tensor("khb", [128, T_], BF16))
            vau = A_(nc.sbuf_tensor("vau", [128, 2 * 4160], BF16))

            def q1f(j):
                return q1[:, j * SEG:(j + 1) * SEG]

            def ybf(j):
                return ybuf[:, j * SEG:(j + 1) * SEG]

            def hsl(buf_f, h):
                r0 = 64 * (h % 2)
                return buf_f(h // 2)[r0:r0 + 64, :]

            def _(s):
                KV.inc(s.dma_start(out=q1[:, :],
                                   in_=q1c.rearrange("(j p) t -> p j t", p=128)), 16)
            SY(_)
            XQ_KV = KV.v

            def _(v):
                v.wait_ge(s_g, LOADED_G)
                DV.inc(v.memset(nmax[:, :], 0.0))
                DV.inc(v.memset(vau[:, :], 0.0))
                v.wait_ge(s_dv, DV.v)
                DV.inc(v.memset(bass.AP(vau, 64, [[2 * 4160, 128], [4160, 2],
                                                  [65, 64]]), 1.0))
            DVE(_)
            INIT_DV = DV.v

            for ps in range(2):
                def _(s, ps=ps):
                    if ps == 0:
                        s.wait_ge(s_dv, INIT_DV)
                    else:
                        s.wait_ge(s_mm, MM.v)
                    W.inc(s.dma_start(
                        out=bass.AP(Pb, 0, [[64 * 512, 128], [T_, 4], [1, T_]]),
                        in_=bass.AP(Kf_in.tensor, 4 * ps * 128 * T_,
                                    [[T_, 128], [128 * T_, 4], [1, T_]])), 16)
                SY(_)
                KP_W = W.v

                for t in range(16):
                    for j in range(4):
                        def _(sc, t=t, j=j, KP_W=KP_W):
                            sc.wait_ge(s_w, KP_W)
                            sc.wait_ge(s_mm, MM.v)
                            AC.inc(sc.activation(
                                ksqt[:, :],
                                bass.AP(Pb, j * T_ + t * 512,
                                        [[64 * 512, 128], [1, 512]]),
                                AF.Square))
                        ACT(_)

                        def _(t2, t=t, j=j, ps=ps):
                            t2.wait_ge(s_ac, AC.v)
                            if j == 0:
                                t2.wait_ge(s_dv, DV.v)
                            jj = 4 * ps + j
                            MM.inc(t2.matmul(
                                pS[0:16, :],
                                e16[0:64, 16 * (2 * jj):16 * (2 * jj) + 16],
                                ksqt[0:64, :], start=(j == 0), stop=(j == 3)))
                            MM.inc(t2.matmul(
                                pR[32:48, :],
                                e16[64:128, 16 * (2 * jj + 1):16 * (2 * jj + 1) + 16],
                                ksqt[64:128, :], start=(j == 0), stop=(j == 3)))
                        PE(_)

                    def _(v, t=t):
                        v.wait_ge(s_mm, MM.v)
                        DV.inc(v.tensor_copy(nscr[:, :], pS[0:16, :]))
                        v.wait_ge(s_dv, DV.v)
                        DV.inc(v.tensor_add(nscr[:, :], nscr[:, :], pR[32:48, :]))
                        v.wait_ge(s_dv, DV.v)
                        DV.inc(v.tensor_reduce(nscr[:, 0:1], nscr[:, :], AX,
                                               ALU.max))
                        v.wait_ge(s_dv, DV.v)
                        DV.inc(v.tensor_max(nmax[:, 0:1], nmax[:, 0:1],
                                            nscr[:, 0:1]))
                    DVE(_)

            for t in range(4):
                for j in range(NJ):
                    def _(sc, t=t, j=j):
                        sc.wait_ge(s_kv, XQ_KV)
                        sc.wait_ge(s_mm, MM.v)
                        AC.inc(sc.activation(ksqt[:, :],
                                             q1f(j)[:, t * 512:t * 512 + 512],
                                             AF.Square))
                    ACT(_)

                    def _(t2, t=t, j=j):
                        t2.wait_ge(s_ac, AC.v)
                        if j == 0:
                            t2.wait_ge(s_dv, DV.v)
                        MM.inc(t2.matmul(
                            pS[0:16, :],
                            e16[0:64, 16 * (2 * j):16 * (2 * j) + 16],
                            ksqt[0:64, :], start=(j == 0), stop=(j == NJ - 1)))
                        MM.inc(t2.matmul(
                            pR[32:48, :],
                            e16[64:128, 16 * (2 * j + 1):16 * (2 * j + 1) + 16],
                            ksqt[64:128, :], start=(j == 0), stop=(j == NJ - 1)))
                    PE(_)

                def _(v, t=t):
                    v.wait_ge(s_mm, MM.v)
                    DV.inc(v.tensor_copy(nscr[:, :], pS[0:16, :]))
                    v.wait_ge(s_dv, DV.v)
                    DV.inc(v.tensor_add(nscr[:, :], nscr[:, :], pR[32:48, :]))
                    v.wait_ge(s_dv, DV.v)
                    DV.inc(v.tensor_reduce(nscr[:, 0:1], nscr[:, :], AX, ALU.max))
                    v.wait_ge(s_dv, DV.v)
                    DV.inc(v.tensor_max(nmax[:, 1:2], nmax[:, 1:2], nscr[:, 0:1]))
                DVE(_)
            QMAX_DV = DV.v

            def _(v):
                v.wait_ge(s_dv, QMAX_DV)
                DV.inc(v.tensor_mul(nmax[:, 0:1], nmax[:, 0:1], nmax[:, 1:2]))
            DVE(_)

            def _(sc):
                sc.wait_ge(s_dv, DV.v)
                AC.inc(sc.activation(nmax[:, 0:1], nmax[:, 0:1], AF.Sqrt))
            ACT(_)

            def _(v):
                v.wait_ge(s_ac, AC.v)
                DV.inc(v.tensor_scalar_mul(nmax[:, 0:1], nmax[:, 0:1], -0.125))
                v.wait_ge(s_dv, DV.v)
                DV.inc(v.tensor_scalar_mul(nscr[:, 0:16], idf[0:16, 0:16],
                                           nmax[0:16, 0:1]))
            DVE(_)
            NU_DV = DV.v

            def _(t):
                t.wait_ge(s_dv, NU_DV)
                MM.inc(t.matmul(pR[:, 0:16], onesf[0:16, :], nscr[:, 0:16],
                                start=True, stop=True))
            PE(_)

            def _(v):
                v.wait_ge(s_mm, MM.v)
                DV.inc(v.tensor_copy(negU[:, 0:16], pR[:, 0:16]))
            DVE(_)
            NEGU_DV = DV.v

            def _(t):
                t.wait_ge(s_dv, NEGU_DV)
                t.wait_ge(s_g, LOADED_G)
                MM.inc(t.matmul(pR[:, 0:256], onesf[0:1, :], mrow[0:1, :],
                                start=True, stop=True))
            PE(_)

            def _(v):
                v.wait_ge(s_mm, MM.v)
                DV.inc(v.tensor_copy(mtbB[:, :], pR[:, 0:256]))
            DVE(_)

            def kvload(h, wait_dv):
                prior = KV.v
                def _(s, h=h, wait_dv=wait_dv, prior=prior):
                    if wait_dv is not None:
                        s.wait_ge(s_dv, wait_dv)
                    else:
                        s.wait_ge(s_mm, MM.v)
                    s.wait_ge(s_kv, prior)
                    r0 = 64 * (h % 2)
                    KV.inc(s.dma_start(out=khb[r0:r0 + 64, :],
                                       in_=Kf_in[64 * h:64 * h + 64, :]), 16)
                    KV.inc(s.dma_start(
                        out=bass.AP(vau, (h % 2) * 4160,
                                    [[2 * 4160, 128], [65, 64], [1, 64]]),
                        in_=bass.AP(vhead_in.tensor, h * 128 * 4096,
                                    [[4096, 128], [64, 64], [1, 64]])), 16)
                SY(_)
                return KV.v

            for_ = att.enter_context(nc.vector.register("for_"))

            kvr = {0: kvload(0, NEGU_DV)}
            kvr[1] = kvload(1, DV.v)

            for h in range(H):
                r0 = 64 * (h % 2)
                kh = khb[r0:r0 + 64, :]
                qh = hsl(q1f, h)
                myrdy = kvr[h]

                def _(v, h=h):
                    v.wait_ge(s_ac, AC.v)
                    v.wait_ge(s_dv, DV.v)
                    DV.inc(v.tensor_scalar_add(mtb[:, :], mtbB[:, :],
                                               negU[:, h:h + 1]))
                DVE(_)

                for qg in range(4):
                    pp = [pA, pB]
                    for kt in range(64):
                        p = pp[kt % 2]

                        def _(t, h=h, qg=qg, kt=kt, p=p, myrdy=myrdy):
                            if kt == 0 and qg == 0:
                                t.wait_ge(s_kv, myrdy)
                            if kt == 0:
                                t.wait_ge(s_dv, DV.v)
                            if kt >= 2:
                                t.wait_ge(s_ac, AC.v - 1)
                            MM.inc(t.matmul(p[:, :],
                                            kh[:, kt * 128:kt * 128 + 128],
                                            qh[:, qg * 512:qg * 512 + 512],
                                            start=True, stop=True))
                        PE(_)

                        def _(sc, h=h, qg=qg, kt=kt, p=p):
                            sc.wait_ge(s_mm, MM.v)
                            AC.inc(sc.activation(
                                Pb[:, kt * 512:kt * 512 + 512], p[:, :],
                                AF.Exp,
                                bias=mtb[:, qg * 64 + kt:qg * 64 + kt + 1],
                                scale=0.125))
                        ACT(_)

                    def _(v, h=h, qg=qg):
                        v.wait_ge(s_ac, AC.v)
                        for fi in range(4):
                            v.reg_load(for_, fixoff[0:1, qg * 10 + fi:
                                                    qg * 10 + fi + 1])
                            DV.inc(v.tensor_mul(
                                bass.AP(Pb, for_, [[64 * 512, 128], [1, 128]]),
                                bass.AP(Pb, for_, [[64 * 512, 128], [1, 128]]),
                                tri01[:, :]))
                        for fi in range(4, 10):
                            v.reg_load(for_, fixoff[0:1, qg * 10 + fi:
                                                    qg * 10 + fi + 1])
                            DV.inc(v.memset(
                                bass.AP(Pb, for_, [[64 * 512, 128], [1, 128]]),
                                0.0))
                    DVE(_)
                    FIX_DV = DV.v

                    def _(t, h=h, qg=qg, FIX_DV=FIX_DV):
                        t.wait_ge(s_dv, FIX_DV)
                        for kt in range(64):
                            MM.inc(t.matmul(
                                pY[:, :],
                                bass.AP(vau, (h % 2) * 4160 + kt * 65,
                                        [[2 * 4160, 128], [1, 65]]),
                                Pb[:, kt * 512:kt * 512 + 512],
                                start=(kt == 0), stop=(kt == 63)))
                    PE(_)
                    PV_MM = MM.v

                    def _(v, PV_MM=PV_MM):
                        v.wait_ge(s_mm, PV_MM)
                        DV.inc(v.reciprocal(lrf[64:65, 0:512], pY[64:65, :]))
                        v.wait_ge(s_dv, DV.v)
                        DV.inc(v.tensor_copy(lrec[64:65, 0:512],
                                             lrf[64:65, 0:512]))
                    DVE(_)
                    REC_DV = DV.v

                    def _(t, REC_DV=REC_DV):
                        t.wait_ge(s_dv, REC_DV)
                        MM.inc(t.matmul(pZ[:, :], ob128[64:65, 0:64],
                                        lrec[64:65, 0:512], start=True,
                                        stop=True))
                    PE(_)
                    BC_MM = MM.v

                    def _(v, h=h, qg=qg, BC_MM=BC_MM):
                        v.wait_ge(s_mm, BC_MM)
                        DV.inc(v.tensor_copy(rb[:, :], pZ[:, :]))
                        v.wait_ge(s_dv, DV.v)
                        DV.inc(v.tensor_mul(ystg[:, :], pY[0:64, :], rb[:, :]))
                        v.wait_ge(s_dv, DV.v)
                        DV.inc(v.tensor_copy(
                            hsl(ybf, h)[:, qg * 512:qg * 512 + 512], ystg[:, :]))
                    DVE(_)

                if h + 2 < H:
                    kvr[h + 2] = kvload(h + 2, None)
            ATT_DV = DV.v

            def _(s):
                s.wait_ge(s_dv, ATT_DV)
                W.inc(s.dma_start(out=ybspill, in_=ybuf[:, :]), 16)
            SY(_)
            YB_W = W.v

            def _(s):
                s.wait_ge(s_w, W.v)
                s.wait_ge(s_kv, KV.v)
                s.wait_ge(s_ac, AC.v)
            SY(_)

        with ExitStack() as accs:
            acc2 = accs.enter_context(nc.sbuf_tensor("accx", [128, NJ * SEG], F32))

            def accf(j):
                return acc2[:, j * SEG:(j + 1) * SEG]

            with ExitStack() as wo1s:
                B_ = wo1s.enter_context
                yb2 = B_(nc.sbuf_tensor("yb2", [128, NJ * SEG], BF16))
                wt0a = B_(nc.sbuf_tensor("wt0a", [128, NJ * 1024], BF16))

                def yb2f(j):
                    return yb2[:, j * SEG:(j + 1) * SEG]

                def _(s):
                    s.wait_ge(s_w, YB_W)
                    W.inc(s.dma_start(out=acc2[:, :],
                                      in_=x1c.rearrange("(j p) t -> p j t", p=128)), 16)
                    W.inc(s.dma_start(out=yb2[:, :], in_=ybspill), 16)
                SY(_)
                ACCL_W = W.v

                def _(g):
                    g.wait_ge(s_w, YB_W)
                    WT.inc(g.dma_start(
                        out=wt0a.ap().rearrange("p (kc o) -> p kc o", kc=NJ),
                        in_=wo1_in.rearrange("(kc p) o -> p kc o", p=128)), 16)
                GP(_)
                WO1_RDY = WT.v

                pp = [pA, pB]
                marksC = []
                idx = 0
                for oc in range(NJ):
                    for (c0, cn) in TOK_TILES:
                        p = pp[idx % 2]

                        def _(t, oc=oc, c0=c0, cn=cn, p=p, i=idx):
                            if i < 2:
                                t.wait_ge(s_wt, WO1_RDY)
                                t.wait_ge(s_w, ACCL_W)
                            else:
                                t.wait_ge(s_dv, marksC[i - 2])
                            for kc in range(NJ):
                                MM.inc(t.matmul(
                                    p[:, 0:cn],
                                    wt0a[:, kc * 1024 + oc * 128:kc * 1024 + oc * 128 + 128],
                                    yb2f(kc)[:, c0:c0 + cn],
                                    start=(kc == 0), stop=(kc == NJ - 1)))
                        PE(_)

                        def _(v, oc=oc, c0=c0, cn=cn, p=p):
                            v.wait_ge(s_mm, MM.v)
                            xo = accf(oc)[:, c0:c0 + cn]
                            DV.inc(v.scalar_tensor_tensor(xo, p[:, 0:cn],
                                                          bo1[:, oc:oc + 1], xo,
                                                          ALU.add, ALU.add))
                            marksC.append(DV.v)
                        DVE(_)
                        idx += 1
                WO1_DV = DV.v

                def _(s):
                    s.wait_ge(s_dv, WO1_DV)
                    s.wait_ge(s_wt, WT.v)
                SY(_)

            with ExitStack() as moes:
                M_ = moes.enter_context
                xn1 = M_(nc.sbuf_tensor("xn1", [128, NJ * SEG], BF16))
                wga = M_(nc.sbuf_tensor("wga", [128, NJ * 1024], BF16))
                wua = M_(nc.sbuf_tensor("wua", [128, NJ * 1024], BF16))
                wda = M_(nc.sbuf_tensor("wda", [128, NJ * 1024], BF16))
                sal = M_(nc.sbuf_tensor("sal", [128, NJ * 512], BF16))
                hpl = M_(nc.sbuf_tensor("hpl", [128, NJ * 512], BF16))
                cbc = M_(nc.sbuf_tensor("cbc", [128, SEG], BF16))

                def xn1f(j):
                    return xn1[:, j * SEG:(j + 1) * SEG]

                def _(s):
                    s.wait_ge(s_dv, WO1_DV)
                    W.inc(s.dma_start(out=xn1[:, :],
                                      in_=xn1c.rearrange("(j p) t -> p j t", p=128)), 16)
                SY(_)
                XN1_W = W.v

                def wload3(kind, e):
                    srcs = {"g": (wg_in, shg_in), "u": (wu_in, shu_in),
                            "d": (wd_in, shd_in)}
                    big, sh = srcs[kind]
                    if e == 16:
                        return sh.rearrange("(kc p) o -> p kc o", p=128)
                    return bass.AP(big.tensor, e * C_ * C_,
                                   [[C_, 128], [128 * C_, NJ], [1, C_]])

                def wswap2(wt_buf, kind, e):
                    prior = WT.v
                    def _(g, wt_buf=wt_buf, kind=kind, e=e, prior=prior):
                        g.wait_ge(s_mm, MM.v)
                        g.wait_ge(s_wt, prior)
                        WT.inc(g.dma_start(
                            out=wt_buf.ap().rearrange("p (kc o) -> p kc o", kc=NJ),
                            in_=wload3(kind, e)), 16)
                    GP(_)
                    return WT.v

                EXPS = list(range(NEXP)) + [16]
                WG_R = wswap2(wga, "g", EXPS[0])
                WU_R = wswap2(wua, "u", EXPS[0])
                WD_R = wswap2(wda, "d", EXPS[0])
                first = True
                for ei, e in enumerate(EXPS):
                    shared = (e == 16)
                    if not shared:
                        for tt, (c0, cn) in enumerate(TOK_TILES):
                            def _(t, e=e, c0=c0, cn=cn):
                                t.wait_ge(s_dv, DV.v)
                                t.wait_ge(s_g, LOADED_G)
                                MM.inc(t.matmul(pC[:, 0:cn],
                                                e16sel[:, e * 128:e * 128 + 128],
                                                cmbb[:, c0:c0 + cn],
                                                start=True, stop=True))
                            PE(_)

                            def _(v, c0=c0, cn=cn):
                                v.wait_ge(s_mm, MM.v)
                                DV.inc(v.tensor_copy(cbc[:, c0:c0 + cn],
                                                     pC[:, 0:cn]))
                            DVE(_)
                    CBC_DV = DV.v

                    for tt, (c0, cn) in enumerate(TOK_TILES):
                        marksH = []
                        for ic in range(NJ):
                            p = [pA, pB][ic % 2]

                            def _(t, ic=ic, c0=c0, cn=cn, p=p, WG_R=WG_R,
                                  first=first, tt=tt):
                                if ic < 2 and tt == 0:
                                    t.wait_ge(s_wt, WG_R)
                                    t.wait_ge(s_dv, DV.v)
                                    if first:
                                        t.wait_ge(s_w, XN1_W)
                                if ic < 2 and tt > 0:
                                    t.wait_ge(s_dv, DV.v)
                                if ic >= 2:
                                    t.wait_ge(s_ac, marksH[ic - 2])
                                for kc in range(NJ):
                                    MM.inc(t.matmul(
                                        p[:, 0:cn],
                                        wga[:, kc * 1024 + ic * 128:kc * 1024 + ic * 128 + 128],
                                        xn1f(kc)[:, c0:c0 + cn],
                                        start=(kc == 0), stop=(kc == NJ - 1)))
                            PE(_)

                            def _(sc, ic=ic, cn=cn, p=p):
                                sc.wait_ge(s_mm, MM.v)
                                AC.inc(sc.activation(sal[:, ic * 512:ic * 512 + cn],
                                                     p[:, 0:cn], AF.Silu))
                                marksH.append(AC.v)
                            ACT(_)
                        HG_AC = AC.v

                        marksU = []
                        for ic in range(NJ):
                            p = [pA, pB][ic % 2]

                            def _(t, ic=ic, c0=c0, cn=cn, p=p, WU_R=WU_R,
                                  HG_AC=HG_AC, CBC_DV=CBC_DV):
                                if ic < 2:
                                    t.wait_ge(s_wt, WU_R)
                                    t.wait_ge(s_ac, HG_AC)
                                    t.wait_ge(s_dv, CBC_DV)
                                else:
                                    t.wait_ge(s_dv, marksU[ic - 2])
                                for kc in range(NJ):
                                    MM.inc(t.matmul(
                                        p[:, 0:cn],
                                        wua[:, kc * 1024 + ic * 128:kc * 1024 + ic * 128 + 128],
                                        xn1f(kc)[:, c0:c0 + cn],
                                        start=(kc == 0), stop=(kc == NJ - 1)))
                            PE(_)

                            def _(v, ic=ic, c0=c0, cn=cn, p=p, shared=shared):
                                v.wait_ge(s_mm, MM.v)
                                if shared:
                                    DV.inc(v.tensor_mul(hpl[:, ic * 512:ic * 512 + cn],
                                                        sal[:, ic * 512:ic * 512 + cn],
                                                        p[:, 0:cn]))
                                else:
                                    v.wait_ge(s_dv, DV.v)
                                    DV.inc(v.tensor_mul(lrec[:, 0:cn],
                                                        sal[:, ic * 512:ic * 512 + cn],
                                                        p[:, 0:cn]))
                                    v.wait_ge(s_dv, DV.v)
                                    DV.inc(v.tensor_mul(hpl[:, ic * 512:ic * 512 + cn],
                                                        lrec[:, 0:cn],
                                                        cbc[:, c0:c0 + cn]))
                                marksU.append(DV.v)
                            DVE(_)
                        HU_DV = DV.v

                        marksD = []
                        for oc in range(NJ):
                            p = [pA, pB][oc % 2]

                            def _(t, oc=oc, c0=c0, cn=cn, p=p, WD_R=WD_R,
                                  HU_DV=HU_DV):
                                if oc < 2:
                                    t.wait_ge(s_wt, WD_R)
                                    t.wait_ge(s_dv, HU_DV)
                                else:
                                    t.wait_ge(s_dv, marksD[oc - 2])
                                for ic in range(NJ):
                                    MM.inc(t.matmul(
                                        p[:, 0:cn],
                                        wda[:, ic * 1024 + oc * 128:ic * 1024 + oc * 128 + 128],
                                        hpl[:, ic * 512:ic * 512 + cn],
                                        start=(ic == 0), stop=(ic == NJ - 1)))
                            PE(_)

                            def _(v, oc=oc, c0=c0, cn=cn, p=p):
                                v.wait_ge(s_mm, MM.v)
                                xo = accf(oc)[:, c0:c0 + cn]
                                DV.inc(v.tensor_add(xo, xo, p[:, 0:cn]))
                                marksD.append(DV.v)
                            DVE(_)
                    first = False
                    if ei + 1 < len(EXPS):
                        WG_R = wswap2(wga, "g", EXPS[ei + 1])
                        WU_R = wswap2(wua, "u", EXPS[ei + 1])
                        WD_R = wswap2(wda, "d", EXPS[ei + 1])

                def _(s):
                    s.wait_ge(s_dv, DV.v)
                    W.inc(s.dma_start(out=outp.rearrange("(j p) t -> p j t", p=128),
                                      in_=acc2[:, :].rearrange("p (j t) -> p j t",
                                                               j=NJ)), 16)
                SY(_)

                for name in debug:
                    def _(g, name=name):
                        g.wait_ge(s_dv, DV.v)
                        G.inc(g.dma_start(
                            out=dbg_out[name].rearrange("(j p) t -> p j t", p=128),
                            in_=ybspill.rearrange("p (j t) -> p j t", j=NJ)), 16)
                    GP(_)

                def _(s):
                    s.wait_ge(s_w, W.v)
                    s.wait_ge(s_g, G.v)
                    s.wait_ge(s_wt, WT.v)
                    s.wait_ge(s_kv, KV.v)
                SY(_)

    if compiled:
        nc.compile()
    return nc


def _rope_cs(seg_off):
    j = (seg_off + np.arange(SEG, dtype=np.float32))[:, None]
    i = np.arange(DH // 2, dtype=np.float32)[None, :]
    theta = (j * np.power(np.float32(10000.0),
                          np.float32(-2.0) * i / np.float32(DH))).astype(np.float32)
    cs = (np.cos(theta) + np.sin(theta)).astype(np.float32)
    cs = np.concatenate([cs, cs], axis=1)
    return np.tile(cs.T, (2, 1))


def _shard_inputs_B(ins, core, a_outs):
    b, seg = core // 4, core % 4
    cc = host_consts()
    comps = np.concatenate([a_outs[4 * b + r]["cbuf"] for r in range(4)], 0)
    flags = np.array([[1.0 if r < seg else 0.0 for r in range(4)]], np.float32)
    bias = np.zeros((128, 24), np.float32)
    bias[:, 0:8] = np.asarray(ins["L1_attn_bq"]).reshape(8, 128).T
    bias[:, 8:16] = np.asarray(ins["L1_attn_bk"]).reshape(8, 128).T
    bias[:, 16:24] = np.asarray(ins["L1_attn_bv"]).reshape(8, 128).T
    d = {
        "x1p_in": a_outs[core]["x1p"],
        "zbuf_in": a_outs[core]["zbuf"],
        "abuf_in": a_outs[core]["abuf"],
        "comps_in": comps,
        "flags_in": flags,
        "woT": np.asarray(ins["L0_attn_Wo"]),
        "wq1T": np.asarray(ins["L1_attn_Wq"]).T,
        "wk1T": np.asarray(ins["L1_attn_Wk"]).T,
        "wv1T": np.asarray(ins["L1_attn_Wv"]).T,
        "gwT": np.asarray(ins["L1_ffn_gate_w"]).T,
        "bias_in": bias,
        "rope2": _rope_cs(seg * SEG).astype(NB),
        "ebias_bc": np.broadcast_to(np.asarray(ins["L1_ffn_expert_bias"]),
                                    (128, 16)),
        "id64x2": np.tile(np.eye(64, dtype=np.float32), (2, 1)),
        "id128f": np.eye(128, dtype=np.float32),
        "ones_in": cc["ones_in"],
    }
    return {k: np.ascontiguousarray(v) for k, v in d.items()}



def _shard_inputs_C(ins, core, b_outs):
    b, seg = core // 4, core % 4
    cc = host_consts()
    Kf = np.concatenate([b_outs[4 * b + r]["k1o"] for r in range(4)], axis=1)
    Vf = np.concatenate([b_outs[4 * b + r]["v1o"] for r in range(4)], axis=1)
    vh = Vf.reshape(16, 64, 64, 128)
    vhead = np.ascontiguousarray(vh.transpose(0, 3, 2, 1).reshape(16 * 128, 4096))
    seg_t = seg * 16
    mrow = np.zeros((1, 256), np.float32)
    fix = np.zeros((1, 40), np.int32)
    for qg in range(4):
        D = seg_t + 4 * qg
        for kt in range(64):
            if kt > D + 3:
                mrow[0, qg * 64 + kt] = -1e30
    for qg in range(4):
        D = seg_t + 4 * qg
        k = 0
        for sp in range(4):
            fix[0, qg * 10 + k] = (D + sp) * 512 + sp * 128
            k += 1
        for spp in range(1, 4):
            for sp in range(spp):
                fix[0, qg * 10 + k] = (D + spp) * 512 + sp * 128
                k += 1
    d = {
        "x1c": b_outs[core]["x1o"],
        "xn1c": b_outs[core]["xn1o"],
        "q1c": b_outs[core]["q1o"],
        "Kf": Kf,
        "vhead": vhead,
        "cmb": b_outs[core]["combo"],
        "wgT3": np.asarray(ins["L1_ffn_w_gate"]).transpose(0, 2, 1).reshape(16 * C_, C_),
        "wuT3": np.asarray(ins["L1_ffn_w_up"]).transpose(0, 2, 1).reshape(16 * C_, C_),
        "wdT3": np.asarray(ins["L1_ffn_w_down"]).transpose(0, 2, 1).reshape(16 * C_, C_),
        "shgT": np.asarray(ins["L1_ffn_shared_gate_w"]).T,
        "shuT": np.asarray(ins["L1_ffn_shared_up_w"]).T,
        "shdT": np.asarray(ins["L1_ffn_shared_down_w"]).T,
        "wo1T": np.asarray(ins["L1_attn_Wo"]).T,
        "bo1c": np.asarray(ins["L1_attn_bo"]).reshape(8, 128).T,
        "tri01": cc["tri01"], "ob128": cc["ob128"], "ones_in": cc["ones_in"],
        "e16": cc["e16"], "mrow": mrow, "fixoff": fix,
        "id128f": cc["id128f"], "e16sel": cc["e16sel"],
    }
    return {k: np.ascontiguousarray(v) for k, v in d.items()}


def kernel(**inputs):
    ins = {k: np.asarray(v) for k, v in inputs.items()}
    cores = list(range(8))

    ncA = build_launch_A()
    rA = run_bass_kernel_spmd(ncA, [_shard_inputs_A(ins, c) for c in cores],
                              core_ids=cores)
    a_outs = [rA.results[c] for c in cores]

    ncB = build_launch_B()
    rB = run_bass_kernel_spmd(ncB, [_shard_inputs_B(ins, c, a_outs)
                                    for c in cores], core_ids=cores)
    b_outs = [rB.results[c] for c in cores]

    ncC = build_launch_C(compiled=True)
    rC = run_bass_kernel_spmd(ncC, [_shard_inputs_C(ins, c, b_outs)
                                    for c in cores], core_ids=cores)

    out = np.empty((B_, T_, C_), np.float32)
    for c in cores:
        b, seg = c // 4, c % 4
        out[b, seg * SEG:(seg + 1) * SEG, :] = \
            np.asarray(rC.results[c]["outp"], np.float32).T
    return out
